# version 05
# baseline (speedup 1.0000x reference)
import numpy as np
import ml_dtypes
import concourse.bass as bass
import concourse.mybir as mybir
from concourse.bass_utils import run_bass_kernel_spmd
from contextlib import ExitStack

F32 = mybir.dt.float32
BF16 = mybir.dt.bfloat16
AF = mybir.ActivationFunctionType
ALU = mybir.AluOpType

P = 128
B, N, D, H = 4, 8192, 2048, 16
HG = 2
HPC = 8
EC = HPC * P
INTER = 5632
NIB = INTER // P
KD_ = D // P

TA = 256
NTA = N // TA
CPT = TA // P
NCH = N // P
TB = 512
TOKB = N // HG
NTB = TOKB // TB
TA2 = 512
NA2 = N // TA2

EPS = 1e-5
PAIRS = [[0, 1], [2, 3], [4, 5], [6, 7]]
NPG = 3 * HPC + 1 + HPC
XG = 4


def build_program():
    nc = bass.Bass("TRN2", target_bir_lowering=False, num_devices=8)

    xT = nc.dram_tensor("xT", [D, N], F32, kind="ExternalInput").ap()
    xTB = nc.dram_tensor("xTB", [D, TOKB], F32, kind="ExternalInput").ap()
    wqd = nc.dram_tensor("wqd", [D, EC], F32, kind="ExternalInput").ap()
    wkd = nc.dram_tensor("wkd", [D, EC], F32, kind="ExternalInput").ap()
    wvd = nc.dram_tensor("wvd", [D, EC], F32, kind="ExternalInput").ap()
    wg1d = nc.dram_tensor("wg1d", [D, P], F32, kind="ExternalInput").ap()
    wg2d = nc.dram_tensor("wg2d", [P, EC], F32, kind="ExternalInput").ap()
    wosd = nc.dram_tensor("wosd", [EC, D], F32, kind="ExternalInput").ap()
    gwTd = nc.dram_tensor("gwTd", [D, INTER], F32, kind="ExternalInput").ap()
    uwTd = nc.dram_tensor("uwTd", [D, INTER], F32, kind="ExternalInput").ap()
    dwTd = nc.dram_tensor("dwTd", [INTER, D], F32, kind="ExternalInput").ap()
    ld8 = nc.dram_tensor("ld8", [HPC], F32, kind="ExternalInput").ap()
    gsd = nc.dram_tensor("gsd", [INTER], F32, kind="ExternalInput").ap()
    usd = nc.dram_tensor("usd", [INTER], F32, kind="ExternalInput").ap()
    dsd = nc.dram_tensor("dsd", [D], F32, kind="ExternalInput").ap()
    nwsd = nc.dram_tensor("nwsd", [EC], F32, kind="ExternalInput").ap()
    cDIFF = nc.dram_tensor("cDIFF", [P, P], F32, kind="ExternalInput").ap()
    cCP1 = nc.dram_tensor("cCP1", [P, P], F32, kind="ExternalInput").ap()
    cKREV = nc.dram_tensor("cKREV", [P, P], F32, kind="ExternalInput").ap()
    cIDT = nc.dram_tensor("cIDT", [P, P], BF16, kind="ExternalInput").ap()

    out_y = nc.dram_tensor("out_y", [D, TOKB], F32, kind="ExternalOutput").ap()

    wo_bf = nc.dram_tensor("wo_bf", [EC, D], BF16, kind="Internal").ap()
    gw_bf = nc.dram_tensor("gw_bf", [D, INTER], BF16, kind="Internal").ap()
    uw_bf = nc.dram_tensor("uw_bf", [D, INTER], BF16, kind="Internal").ap()
    dw_bf = nc.dram_tensor("dw_bf", [INTER, D], BF16, kind="Internal").ap()
    onF = nc.dram_tensor("onF", [EC, N], BF16, kind="Internal").ap()
    cc_in = nc.dram_tensor("cc_in", [2 * D, TOKB], F32, kind="Internal").ap()
    cc_y1 = nc.dram_tensor("cc_y1", [D, TOKB], F32, kind="Internal").ap()

    es = ExitStack()

    semnames = [
        "sct", "sx", "sxf", "ssqx", "ssm", "svt", "srt", "sxn", "swq", "sldm", "sldb",
        "spm", "spe",
        "samm", "saev", "somm", "sogev", "sogsq", "ssmm2", "sgn", "sonm", "ssup",
        "sost0", "sost1",
        "swl0", "swl1", "gpl0", "gpl1", "gps0", "gps1", "s_wo", "s_gu", "s_dw",
        "barA", "barA2",
        "sa2w", "sa2x0", "sa2x1", "sa2mm", "sa2ev", "sa2st0", "sa2st1",
        "srs",
        "sby", "sbx0", "sbx1", "sbx2", "sbx3", "sbr1", "sbcast", "ssqB",
        "sbsm", "sbvt", "sbrt", "sbxn",
        "sbwl0", "sbwl1", "sbgm", "sbsg", "sbhh", "sbdl0", "sbdl1", "sbdm",
        "sbyev", "sbst0", "sbst1",
    ]
    S = {nm: es.enter_context(nc.semaphore(nm)) for nm in semnames}

    ones_bf = es.enter_context(nc.sbuf_tensor("ones_bf", [P, P], BF16))
    onesf_row = es.enter_context(nc.sbuf_tensor("onesf_row", [1, P], F32))
    ld_row = es.enter_context(nc.sbuf_tensor("ld_row", [1, HPC], F32))
    c_eps = es.enter_context(nc.sbuf_tensor("c_eps", [P, 1], F32))
    c_ln05 = es.enter_context(nc.sbuf_tensor("c_ln05", [P, 1], F32))
    LDB = es.enter_context(nc.sbuf_tensor("LDB", [P, HPC], F32))
    MT = es.enter_context(nc.sbuf_tensor("MT", [P, EC], BF16))
    QD = es.enter_context(nc.sbuf_tensor("QD", [P, EC], BF16))
    KDt = es.enter_context(nc.sbuf_tensor("KDt", [P, EC], BF16))
    CDt = es.enter_context(nc.sbuf_tensor("CDt", [P, EC], F32))
    idt = es.enter_context(nc.sbuf_tensor("idt", [P, P], BF16))
    gs_col = es.enter_context(nc.sbuf_tensor("gs_col", [P, NIB], F32))
    us_col = es.enter_context(nc.sbuf_tensor("us_col", [P, NIB], F32))
    ds_col = es.enter_context(nc.sbuf_tensor("ds_col", [P, KD_], F32))
    nw_col = es.enter_context(nc.sbuf_tensor("nw_col", [P, HPC], F32))
    c128t = es.enter_context(nc.sbuf_tensor("c128t", [P, P], F32))
    dsc = es.enter_context(nc.sbuf_tensor("dsc", [P, 3 * P], F32))
    WPW = 1024
    stw = [es.enter_context(nc.sbuf_tensor(f"stw{i}", [P, WPW], F32)) for i in range(2)]
    stwb = [es.enter_context(nc.sbuf_tensor(f"stwb{i}", [P, WPW], BF16)) for i in range(2)]

    prep_jobs = []

    def add_prep(src, dst, rows, cols, nwfold=False):
        for r in range(rows // P):
            c0 = 0
            while c0 < cols:
                w = min(WPW, cols - c0)
                prep_jobs.append((
                    src[r * P:(r + 1) * P, c0:c0 + w],
                    dst[r * P:(r + 1) * P, c0:c0 + w],
                    (nw_col[:, r:r + 1] if nwfold else None),
                    w,
                ))
                c0 += w

    add_prep(wosd, wo_bf, EC, D, nwfold=True)
    n_wo_jobs = len(prep_jobs)
    add_prep(gwTd, gw_bf, D, INTER)
    add_prep(uwTd, uw_bf, D, INTER)
    n_gu_jobs = len(prep_jobs)
    add_prep(dwTd, dw_bf, INTER, D)
    n_all_jobs = len(prep_jobs)
    n_wslab = 3 * KD_ + KD_ + 1

    esA = ExitStack()
    wq_bf = esA.enter_context(nc.sbuf_tensor("wq_bf", [P, KD_ * EC], BF16))
    wk_bf = esA.enter_context(nc.sbuf_tensor("wk_bf", [P, KD_ * EC], BF16))
    wv_bf = esA.enter_context(nc.sbuf_tensor("wv_bf", [P, KD_ * EC], BF16))
    wg1_bf = esA.enter_context(nc.sbuf_tensor("wg1_bf", [P, KD_ * P], BF16))
    wg2_bf = esA.enter_context(nc.sbuf_tensor("wg2_bf", [P, EC], BF16))
    stag = esA.enter_context(nc.sbuf_tensor("stag", [P, 4 * TA], F32))
    xbf = esA.enter_context(nc.sbuf_tensor("xbf", [P, KD_ * TA], BF16))
    sqb = esA.enter_context(nc.sbuf_tensor("sqb", [P, KD_ * TA], BF16))
    xn2 = [esA.enter_context(nc.sbuf_tensor(f"xn{i}", [P, KD_ * TA], BF16)) for i in range(2)]
    qbf = esA.enter_context(nc.sbuf_tensor("qbf", [P, HPC * TA], BF16))
    kbf = esA.enter_context(nc.sbuf_tensor("kbf", [P, HPC * TA], BF16))
    vbf = esA.enter_context(nc.sbuf_tensor("vbf", [P, HPC * TA], BF16))
    g1bf = esA.enter_context(nc.sbuf_tensor("g1bf", [P, TA], BF16))
    tgbf = esA.enter_context(nc.sbuf_tensor("tgbf", [P, HPC * TA], BF16))
    mt_s = esA.enter_context(nc.sbuf_tensor("mt_s", [P, TA], F32))
    vt_s = esA.enter_context(nc.sbuf_tensor("vt_s", [P, TA], F32))
    rt_s = esA.enter_context(nc.sbuf_tensor("rt_s", [P, TA], F32))
    tmp_s = esA.enter_context(nc.sbuf_tensor("tmp_s", [P, TA], F32))
    scT = esA.enter_context(nc.sbuf_tensor("scT", [P, EC], BF16))
    vt_bf = esA.enter_context(nc.sbuf_tensor("vt_bf", [P, EC], BF16))
    kt_bf = esA.enter_context(nc.sbuf_tensor("kt_bf", [P, EC], BF16))
    qp_bf = esA.enter_context(nc.sbuf_tensor("qp_bf", [P, EC], BF16))
    onormbf = [esA.enter_context(nc.sbuf_tensor(f"onormbf{i}", [P, EC], BF16))
               for i in range(2)]
    St = esA.enter_context(nc.sbuf_tensor("St", [P, EC], F32))
    S_bf = esA.enter_context(nc.sbuf_tensor("S_bf", [P, EC], BF16))
    ogbf = vt_bf
    ogsq = kt_bf

    psA = esA.enter_context(nc.psum_tensor("psA", [P, EC], F32))
    psBt = esA.enter_context(nc.psum_tensor("psBt", [P, 2 * EC], BF16))
    psC = esA.enter_context(nc.psum_tensor("psC", [P, EC], F32))
    psP = [esA.enter_context(nc.psum_tensor(f"psP{i}", [P, 512], F32)) for i in range(2)]

    def hc(t):
        return t[:].rearrange("p (h c) -> p h c", h=HPC)

    def xsrc(i, g):
        return xT[512 * g:512 * (g + 1), i * TA:(i + 1) * TA].rearrange(
            "(k p) t -> p k t", p=P)

    blk = es.enter_context(nc.Block())

    NCONST = 9

    def sx_tgt(i, g):
        return 16 * (XG * i + g + 1)

    @blk.sync
    def _(sync):
        sync.dma_start(out=dsc[:, 0:P], in_=cDIFF).then_inc(S["sct"], 16)
        sync.dma_start(out=dsc[:, P:2 * P], in_=cCP1).then_inc(S["sct"], 16)
        sync.dma_start(out=dsc[:, 2 * P:3 * P], in_=cKREV).then_inc(S["sct"], 16)
        sync.dma_start(out=idt[:], in_=cIDT).then_inc(S["sct"], 16)
        sync.dma_start(out=ld_row[:], in_=ld8.rearrange("(one h) -> one h", one=1)
                       ).then_inc(S["sct"], 16)
        sync.dma_start(out=gs_col[:], in_=gsd.rearrange("(i p) -> p i", p=P),
                       allow_slow_non_contiguous=True).then_inc(S["sct"], 16)
        sync.dma_start(out=us_col[:], in_=usd.rearrange("(i p) -> p i", p=P),
                       allow_slow_non_contiguous=True).then_inc(S["sct"], 16)
        sync.dma_start(out=ds_col[:], in_=dsd.rearrange("(i p) -> p i", p=P),
                       allow_slow_non_contiguous=True).then_inc(S["sct"], 16)
        sync.dma_start(out=nw_col[:], in_=nwsd.rearrange("(i p) -> p i", p=P),
                       allow_slow_non_contiguous=True).then_inc(S["sct"], 16)
        for g in range(XG):
            if XG * 0 + g >= 1:
                sync.wait_ge(S["sxf"], XG * 0 + g)
            sync.dma_start(out=stag[:].rearrange("p (k t) -> p k t", k=4),
                           in_=xsrc(0, g)).then_inc(S["sx"], 16)
        wslabs = []
        for w in (wqd, wkd, wvd):
            for r in range(KD_):
                wslabs.append((w[r * P:(r + 1) * P, :], EC))
        for r in range(KD_):
            wslabs.append((wg1d[r * P:(r + 1) * P, :], P))
        wslabs.append((wg2d[:, :], EC))
        for j, (src, cols) in enumerate(wslabs):
            if j >= 2:
                sync.wait_ge(S["swq"], j - 1)
            sync.dma_start(out=stw[j % 2][:, 0:cols], in_=src).then_inc(
                S["swl1" if j % 2 else "swl0"], 16)
        for i in range(1, NTA):
            for g in range(XG):
                sync.wait_ge(S["sxf"], XG * i + g)
                sync.dma_start(out=stag[:].rearrange("p (k t) -> p k t", k=4),
                               in_=xsrc(i, g)).then_inc(S["sx"], 16)

    @blk.vector
    def _(v):
        v.memset(ones_bf[:], 1.0)
        v.memset(onesf_row[:], 1.0)
        v.memset(c_eps[:], EPS)
        v.memset(c_ln05[:], float(np.log(0.5)))
        v.memset(c128t[:], float(P))
        v.memset(St[:], 0.0)
        v.memset(S_bf[:], 0.0).then_inc(S["ssup"], 1)
        for g in range(XG):
            v.wait_ge(S["sx"], sx_tgt(0, g))
            for k in range(4):
                ki = 4 * g + k
                ins = v.tensor_copy(xbf[:, ki * TA:(ki + 1) * TA],
                                    stag[:, k * TA:(k + 1) * TA])
            ins.then_inc(S["sxf"], 1)
        wdst = []
        for wt in (wq_bf, wk_bf, wv_bf):
            for r in range(KD_):
                wdst.append((wt[:, r * EC:(r + 1) * EC], EC))
        for r in range(KD_):
            wdst.append((wg1_bf[:, r * P:(r + 1) * P], P))
        wdst.append((wg2_bf[:, 0:EC], EC))
        for j, (dst, cols) in enumerate(wdst):
            v.wait_ge(S["swl1" if j % 2 else "swl0"], 16 * (j // 2 + 1))
            v.tensor_copy(dst, stw[j % 2][:, 0:cols]).then_inc(S["swq"], 1)
        v.wait_ge(S["ssm"], 1)
        v.tensor_scalar(mt_s[:], psP[0][:, 0:TA], 1.0 / D, None, ALU.mult)
        v.tensor_tensor(tmp_s[:], mt_s[:], mt_s[:], ALU.mult)
        v.scalar_tensor_tensor(vt_s[:], psP[1][:, 0:TA], 1.0 / D, tmp_s[:],
                               ALU.mult, ALU.subtract).then_inc(S["svt"], 1)
        v.wait_ge(S["srt"], 1)
        for k in range(KD_):
            v.tensor_tensor(tmp_s[:], xbf[:, k * TA:(k + 1) * TA], mt_s[:], ALU.subtract)
            ins = v.tensor_tensor(xn2[0][:, k * TA:(k + 1) * TA], tmp_s[:], rt_s[:],
                                  ALU.mult)
        ins.then_inc(S["sxn"], 1)

        for i in range(NTA):
            if i < NTA - 1:
                v.wait_ge(S["ssm"], i + 1)
                for g in range(XG):
                    v.wait_ge(S["sx"], sx_tgt(i + 1, g))
                    for k in range(4):
                        ki = 4 * g + k
                        ins = v.tensor_copy(xbf[:, ki * TA:(ki + 1) * TA],
                                            stag[:, k * TA:(k + 1) * TA])
                    ins.then_inc(S["sxf"], 1)
            for cc in range(CPT):
                n = CPT * i + cc
                v.wait_ge(S["samm"], n + 1)
                v.tensor_tensor(scT[:], psA[:], MT[:], ALU.mult)
                v.tensor_copy(vt_bf[:], psBt[:, 0:EC])
                v.tensor_tensor(kt_bf[:], psBt[:, EC:2 * EC], KDt[:], ALU.mult)
                v.tensor_tensor(
                    hc(qp_bf),
                    qbf[:].rearrange("p (h t) -> p h t", h=HPC)[:, :, cc * P:(cc + 1) * P],
                    hc(QD), ALU.mult).then_inc(S["saev"], 1)
                v.wait_ge(S["somm"], n + 1)
                v.scalar_tensor_tensor(
                    hc(ogbf),
                    tgbf[:].rearrange("p (h t) -> p h t", h=HPC)[:, :, cc * P:(cc + 1) * P],
                    1.0, hc(psA), ALU.add, ALU.mult).then_inc(S["sogev"], 1)
                v.tensor_tensor(St[:], St[:], CDt[:], ALU.mult)
                v.tensor_tensor(St[:], St[:], psC[:], ALU.add)
                v.tensor_copy(S_bf[:], St[:]).then_inc(S["ssup"], 1)
                v.wait_ge(S["sgn"], n + 1)
                if n >= 2:
                    v.wait_ge(S["sost1" if n % 2 else "sost0"], 16 * (n // 2))
                v.tensor_tensor(onormbf[n % 2][:], ogbf[:], psC[:],
                                ALU.mult).then_inc(S["sonm"], 1)
            if i < NTA - 1:
                v.wait_ge(S["ssm"], i + 2)
                v.tensor_scalar(mt_s[:], psP[0][:, 0:TA], 1.0 / D, None, ALU.mult)
                v.tensor_tensor(tmp_s[:], mt_s[:], mt_s[:], ALU.mult)
                v.scalar_tensor_tensor(vt_s[:], psP[1][:, 0:TA], 1.0 / D, tmp_s[:],
                                       ALU.mult, ALU.subtract).then_inc(S["svt"], 1)
                v.wait_ge(S["srt"], i + 2)
                for k in range(KD_):
                    v.tensor_tensor(tmp_s[:], xbf[:, k * TA:(k + 1) * TA], mt_s[:],
                                    ALU.subtract)
                    ins = v.tensor_tensor(xn2[(i + 1) % 2][:, k * TA:(k + 1) * TA],
                                          tmp_s[:], rt_s[:], ALU.mult)
                ins.then_inc(S["sxn"], 1)
        v.wait_ge(S["sost0"], 16 * (NCH // 2))
        v.wait_ge(S["sost1"], 16 * (NCH // 2))
        v.tensor_copy(mt_s[:, 0:1], mt_s[:, 0:1]).then_inc(S["barA"], 1)

    @blk.tensor
    def _(t):
        t.wait_ge(S["sct"], 16 * 9)
        t.wait_ge(S["ssup"], 1)
        t.matmul(psP[0][:, 0:HPC], onesf_row[:], ld_row[:],
                 start=True, stop=True).then_inc(S["sldm"], 1)
        t.wait_ge(S["sldb"], 1)
        t.wait_ge(S["sxf"], XG)
        t.wait_ge(S["ssqx"], 1)
        for k in range(KD_):
            t.matmul(psP[0][:, 0:TA], ones_bf[:], xbf[:, k * TA:(k + 1) * TA],
                     start=(k == 0), stop=(k == KD_ - 1))
            ins = t.matmul(psP[1][:, 0:TA], ones_bf[:], sqb[:, k * TA:(k + 1) * TA],
                           start=(k == 0), stop=(k == KD_ - 1))
        ins.then_inc(S["ssm"], 1)
        t.wait_ge(S["swq"], n_wslab)

        for i in range(NTA):
            t.wait_ge(S["sxn"], i + 1)
            g = 0
            for wt in (wq_bf, wk_bf, wv_bf):
                for e in range(HPC):
                    if 33 * i + g - 1 >= 1:
                        t.wait_ge(S["spe"], 33 * i + g - 1)
                    if g < 2:
                        t.wait_ge(S["svt"], i + 1)
                    for k in range(KD_):
                        ins = t.matmul(psP[g % 2][:, 0:TA],
                                       wt[:, k * EC + e * P:k * EC + (e + 1) * P],
                                       xn2[i % 2][:, k * TA:(k + 1) * TA],
                                       start=(k == 0), stop=(k == KD_ - 1))
                    ins.then_inc(S["spm"], 1)
                    g += 1
            t.wait_ge(S["spe"], 33 * i + g - 1)
            for k in range(KD_):
                ins = t.matmul(psP[g % 2][:, 0:TA],
                               wg1_bf[:, k * P:(k + 1) * P],
                               xn2[i % 2][:, k * TA:(k + 1) * TA],
                               start=(k == 0), stop=(k == KD_ - 1))
            ins.then_inc(S["spm"], 1)
            g += 1
            for e in range(HPC):
                t.wait_ge(S["spe"], max(33 * i + g - 1, 33 * i + 25))
                t.matmul(psP[g % 2][:, 0:TA],
                         wg2_bf[:, e * P:(e + 1) * P], g1bf[:],
                         start=True, stop=True).then_inc(S["spm"], 1)
                g += 1

            for cc in range(CPT):
                n = CPT * i + cc
                if cc == 0:
                    t.wait_ge(S["spe"], 33 * (i + 1))
                if n >= 1:
                    t.wait_ge(S["sgn"], n)
                for h in range(HPC):
                    t.matmul(psA[:, h * P:(h + 1) * P],
                             kbf[:, h * TA + cc * P:h * TA + (cc + 1) * P],
                             qbf[:, h * TA + cc * P:h * TA + (cc + 1) * P],
                             start=True, stop=True)
                for h in range(HPC):
                    t.transpose(psBt[:, h * P:(h + 1) * P],
                                vbf[:, h * TA + cc * P:h * TA + (cc + 1) * P], idt[:])
                for h in range(HPC):
                    ins = t.transpose(psBt[:, EC + h * P:EC + (h + 1) * P],
                                      kbf[:, h * TA + cc * P:h * TA + (cc + 1) * P],
                                      idt[:])
                ins.then_inc(S["samm"], 1)
                t.wait_ge(S["saev"], n + 1)
                t.wait_ge(S["ssup"], n + 1)
                if n >= 1:
                    t.wait_ge(S["sonm"], n)
                for h in range(HPC):
                    t.matmul(psA[:, h * P:(h + 1) * P],
                             vt_bf[:, h * P:(h + 1) * P], scT[:, h * P:(h + 1) * P],
                             start=True, stop=False)
                    t.matmul(psA[:, h * P:(h + 1) * P],
                             S_bf[:, h * P:(h + 1) * P], qp_bf[:, h * P:(h + 1) * P],
                             start=False, stop=True)
                for h in range(HPC):
                    ins = t.matmul(psC[:, h * P:(h + 1) * P],
                                   kt_bf[:, h * P:(h + 1) * P],
                                   vt_bf[:, h * P:(h + 1) * P],
                                   start=True, stop=True)
                ins.then_inc(S["somm"], 1)
                t.wait_ge(S["sogsq"], n + 1)
                t.matmul(psA[:, 0:512], ones_bf[:], ogsq[:, 0:512], start=True, stop=True)
                t.matmul(psA[:, 512:EC], ones_bf[:], ogsq[:, 512:EC],
                         start=True, stop=True).then_inc(S["ssmm2"], 1)

            if i < NTA - 1:
                t.wait_ge(S["sxf"], XG * (i + 2))
                t.wait_ge(S["ssqx"], i + 2)
                t.wait_ge(S["svt"], i + 1)
                t.wait_ge(S["spe"], 33 * (i + 1))
                for k in range(KD_):
                    t.matmul(psP[0][:, 0:TA], ones_bf[:], xbf[:, k * TA:(k + 1) * TA],
                             start=(k == 0), stop=(k == KD_ - 1))
                    ins = t.matmul(psP[1][:, 0:TA], ones_bf[:],
                                   sqb[:, k * TA:(k + 1) * TA],
                                   start=(k == 0), stop=(k == KD_ - 1))
                ins.then_inc(S["ssm"], 1)
        t.wait_ge(S["ssmm2"], NCH)
        t.matmul(psP[0][:, 0:P], ones_bf[:], ones_bf[:],
                 start=True, stop=True).then_inc(S["barA"], 1)

    @blk.scalar
    def _(sc):
        sc.wait_ge(S["sldm"], 1)
        sc.copy(LDB[:], psP[0][:, 0:HPC]).then_inc(S["sldb"], 1)
        DIFF = dsc[:, 0:P]
        CP1 = dsc[:, P:2 * P]
        KREV = dsc[:, 2 * P:3 * P]
        for h in range(HPC):
            sc.activation(MT[:, h * P:(h + 1) * P], DIFF, AF.Exp, scale=LDB[:, h:h + 1])
            sc.activation(QD[:, h * P:(h + 1) * P], CP1, AF.Exp, scale=LDB[:, h:h + 1])
            sc.activation(KDt[:, h * P:(h + 1) * P], KREV, AF.Exp, scale=LDB[:, h:h + 1])
            sc.activation(CDt[:, h * P:(h + 1) * P], c128t[:], AF.Exp,
                          scale=LDB[:, h:h + 1])
        sc.wait_ge(S["svt"], 1)
        sc.activation(rt_s[:], vt_s[:], AF.Ln, bias=c_eps[:, 0:1])
        sc.activation(rt_s[:], rt_s[:], AF.Exp, scale=-0.5).then_inc(S["srt"], 1)

        for i in range(NTA):
            g = 0
            for dst in (qbf, kbf, vbf):
                for e in range(HPC):
                    sc.wait_ge(S["spm"], 33 * i + g + 1)
                    sc.activation(dst[:, e * TA:(e + 1) * TA], psP[g % 2][:, 0:TA],
                                  AF.Silu).then_inc(S["spe"], 1)
                    g += 1
            sc.wait_ge(S["spm"], 33 * i + g + 1)
            sc.copy(g1bf[:], psP[g % 2][:, 0:TA]).then_inc(S["spe"], 1)
            g += 1
            for e in range(HPC):
                sc.wait_ge(S["spm"], 33 * i + g + 1)
                sc.activation(tgbf[:, e * TA:(e + 1) * TA], psP[g % 2][:, 0:TA],
                              AF.Tanh, scale=0.5).then_inc(S["spe"], 1)
                g += 1
            for cc in range(CPT):
                n = CPT * i + cc
                sc.wait_ge(S["ssmm2"], n + 1)
                sc.wait_ge(S["ssup"], n + 2)
                sc.activation(psC[:], psA[:], AF.Ln, bias=c_eps[:, 0:1], scale=1.0 / 512.0)
                sc.activation(psC[:], psC[:], AF.Exp, bias=c_ln05[:, 0:1],
                              scale=-0.5).then_inc(S["sgn"], 1)
                sc.wait_ge(S["sonm"], n + 1)
                sc.dma_start(
                    out=onF.rearrange("(h e) t -> e h t", e=P)[:, :, n * P:(n + 1) * P],
                    in_=onormbf[n % 2][:].rearrange("e (h c) -> e h c", h=HPC),
                ).then_inc(S["sost1" if n % 2 else "sost0"], 16)
            if i < NTA - 1:
                sc.wait_ge(S["svt"], i + 2)
                sc.activation(rt_s[:], vt_s[:], AF.Ln, bias=c_eps[:, 0:1])
                sc.activation(rt_s[:], rt_s[:], AF.Exp, scale=-0.5).then_inc(S["srt"], 1)
        sc.wait_ge(S["sost0"], 16 * (NCH // 2))
        sc.wait_ge(S["sost1"], 16 * (NCH // 2))
        sc.copy(LDB[:, 0:1], LDB[:, 0:1]).then_inc(S["barA"], 1)

    @blk.gpsimd
    def _(gp):
        gp.wait_ge(S["sxf"], XG)
        for k in range(KD_):
            ins = gp.tensor_tensor(sqb[:, k * TA:(k + 1) * TA],
                                   xbf[:, k * TA:(k + 1) * TA],
                                   xbf[:, k * TA:(k + 1) * TA], ALU.mult)
        ins.then_inc(S["ssqx"], 1)

        gp.wait_ge(S["swq"], n_wslab)
        state = {"j": 0, "issued": 0}

        def issue_load():
            jx = state["issued"]
            if jx < n_all_jobs:
                src, dst, nwf, w = prep_jobs[jx]
                gp.dma_start(out=stw[jx % 2][:, 0:w], in_=src).then_inc(
                    S["gpl1" if jx % 2 else "gpl0"], 16)
                state["issued"] = jx + 1

        def prep_slice(maxjobs):
            for _ in range(maxjobs):
                jx = state["j"]
                if jx >= n_all_jobs:
                    return
                while state["issued"] <= jx + 1 and state["issued"] < n_all_jobs:
                    issue_load()
                src, dst, nwf, w = prep_jobs[jx]
                gp.wait_ge(S["gpl1" if jx % 2 else "gpl0"], 16 * (jx // 2 + 1))
                if jx >= 2:
                    gp.wait_ge(S["gps1" if jx % 2 else "gps0"], 16 * (jx // 2))
                if nwf is not None:
                    gp.tensor_scalar(stwb[jx % 2][:, 0:w], stw[jx % 2][:, 0:w], nwf,
                                     None, ALU.mult)
                else:
                    gp.tensor_copy(stwb[jx % 2][:, 0:w], stw[jx % 2][:, 0:w])
                gp.dma_start(out=dst, in_=stwb[jx % 2][:, 0:w]).then_inc(
                    S["gps1" if jx % 2 else "gps0"], 16)
                state["j"] = jx + 1
                if state["j"] == n_wo_jobs:
                    gp.tensor_copy(c128t[:, 0:1], c128t[:, 0:1]).then_inc(S["s_wo"], 1)
                if state["j"] == n_gu_jobs:
                    gp.tensor_copy(c128t[:, 0:1], c128t[:, 0:1]).then_inc(S["s_gu"], 1)
                if state["j"] == n_all_jobs:
                    gp.wait_ge(S["gps0"], 16 * ((n_all_jobs + 1) // 2))
                    gp.wait_ge(S["gps1"], 16 * (n_all_jobs // 2))
                    gp.tensor_copy(c128t[:, 0:1], c128t[:, 0:1]).then_inc(S["s_dw"], 1)

        for i in range(NTA):
            if i < NTA - 1:
                gp.wait_ge(S["sxf"], XG * (i + 2))
                for k in range(KD_):
                    ins = gp.tensor_tensor(sqb[:, k * TA:(k + 1) * TA],
                                           xbf[:, k * TA:(k + 1) * TA],
                                           xbf[:, k * TA:(k + 1) * TA], ALU.mult)
                ins.then_inc(S["ssqx"], 1)
            for cc in range(CPT):
                n = CPT * i + cc
                gp.wait_ge(S["sogev"], n + 1)
                gp.tensor_tensor(ogsq[:], ogbf[:], ogbf[:], ALU.mult).then_inc(
                    S["sogsq"], 1)
            prep_slice(4)
        prep_slice(n_all_jobs)
        gp.tensor_copy(c128t[:, 0:1], c128t[:, 0:1]).then_inc(S["barA"], 1)

    esA.close()
    esA2 = ExitStack()
    wo2_bf = esA2.enter_context(nc.sbuf_tensor("wo2_bf", [P, HPC * D], BF16))
    on_t = [esA2.enter_context(nc.sbuf_tensor(f"on_t{i}", [P, HPC * TA2], BF16))
            for i in range(2)]
    y1ev = [esA2.enter_context(nc.sbuf_tensor(f"y1ev{i}", [P, TA2], F32))
            for i in range(2)]
    psY2 = [esA2.enter_context(nc.psum_tensor(f"psY2{i}", [P, TA2], F32))
            for i in range(2)]

    @blk.sync
    def _(sync):
        sync.wait_ge(S["barA"], 4)
        sync.wait_ge(S["s_wo"], 1)
        sync.dma_start(out=wo2_bf[:].rearrange("p (k e) -> p k e", k=HPC),
                       in_=wo_bf.rearrange("(k p) e -> p k e", p=P)).then_inc(S["sa2w"], 16)
        for j in range(NA2):
            if j >= 2:
                sync.wait_ge(S["sa2mm"], KD_ * (j - 1))
            sync.dma_start(
                out=on_t[j % 2][:].rearrange("p (k t) -> p k t", k=HPC),
                in_=onF.rearrange("(k p) t -> p k t", p=P)[:, :, j * TA2:(j + 1) * TA2],
            ).then_inc(S["sa2x1" if j % 2 else "sa2x0"], 16)

    @blk.tensor
    def _(t):
        t.wait_ge(S["barA"], 4)
        t.wait_ge(S["sa2w"], 16)
        for j in range(NA2):
            t.wait_ge(S["sa2x1" if j % 2 else "sa2x0"], 16 * (j // 2 + 1))
            for e in range(KD_):
                g = KD_ * j + e
                if g >= 2:
                    t.wait_ge(S["sa2ev"], g - 1)
                for k in range(HPC):
                    ins = t.matmul(psY2[e % 2][:],
                                   wo2_bf[:, k * D + e * P:k * D + (e + 1) * P],
                                   on_t[j % 2][:, k * TA2:(k + 1) * TA2],
                                   start=(k == 0), stop=(k == HPC - 1))
                ins.then_inc(S["sa2mm"], 1)

    @blk.vector
    def _(v):
        v.wait_ge(S["barA"], 4)
        for j in range(NA2):
            for e in range(KD_):
                g = KD_ * j + e
                v.wait_ge(S["sa2mm"], g + 1)
                if g >= 2:
                    v.wait_ge(S["sa2st1" if g % 2 else "sa2st0"], 16 * (g // 2))
                v.tensor_copy(y1ev[e % 2][:], psY2[e % 2][:]).then_inc(S["sa2ev"], 1)

    @blk.scalar
    def _(sc):
        sc.wait_ge(S["barA"], 4)
        for j in range(NA2):
            hrow = (j * TA2) // TOKB
            tl = (j * TA2) % TOKB
            for e in range(KD_):
                g = KD_ * j + e
                sc.wait_ge(S["sa2ev"], g + 1)
                sc.dma_start(
                    out=cc_in[hrow * D + e * P:hrow * D + (e + 1) * P, tl:tl + TA2],
                    in_=y1ev[e % 2][:],
                ).then_inc(S["sa2st1" if g % 2 else "sa2st0"], 16)
        sc.wait_ge(S["sa2st0"], 16 * (NA2 * KD_ // 2))
        sc.wait_ge(S["sa2st1"], 16 * (NA2 * KD_ // 2))
        sc.copy(LDB[:, 0:1], LDB[:, 0:1]).then_inc(S["barA2"], 1)

    @blk.gpsimd
    def _(gp):
        gp.wait_ge(S["sa2st0"], 16 * (NA2 * KD_ // 2))
        gp.wait_ge(S["sa2st1"], 16 * (NA2 * KD_ // 2))
        gp.collective_compute(
            "ReduceScatter", ALU.add, replica_groups=PAIRS,
            ins=[cc_in], outs=[cc_y1],
        ).then_inc(S["srs"], 1)
        gp.wait_ge(S["srs"], 1)
        gp.tensor_copy(c128t[:, 0:1], c128t[:, 0:1]).then_inc(S["barA2"], 1)

    @blk.tensor
    def _(t):
        t.wait_ge(S["sa2mm"], NA2 * KD_)
        t.matmul(psY2[0][:, 0:P], ones_bf[:], ones_bf[:],
                 start=True, stop=True).then_inc(S["barA2"], 1)

    @blk.vector
    def _(v):
        v.wait_ge(S["sa2ev"], NA2 * KD_)
        v.tensor_copy(c128t[:, 0:1], c128t[:, 0:1]).then_inc(S["barA2"], 1)

    esA2.close()
    esB = ExitStack()
    y1t = esB.enter_context(nc.sbuf_tensor("y1t", [P, KD_ * TB], F32))
    xc = [esB.enter_context(nc.sbuf_tensor(f"xc{i}", [P, TB], F32)) for i in range(4)]
    r1bf = esB.enter_context(nc.sbuf_tensor("r1bf", [P, KD_ * TB], BF16))
    sqB = esB.enter_context(nc.sbuf_tensor("sqB", [P, KD_ * TB], BF16))
    xn1 = r1bf
    hh = esB.enter_context(nc.sbuf_tensor("hh", [P, NIB * TB], BF16))
    gw_t = [esB.enter_context(nc.sbuf_tensor(f"gw_t{i}", [P, KD_ * P], BF16))
            for i in range(2)]
    uw_t = [esB.enter_context(nc.sbuf_tensor(f"uw_t{i}", [P, KD_ * P], BF16))
            for i in range(2)]
    dw_t = [esB.enter_context(nc.sbuf_tensor(f"dw_t{i}", [P, NIB * P], BF16))
            for i in range(2)]
    sgbf = [esB.enter_context(nc.sbuf_tensor(f"sgbf{i}", [P, TB], BF16)) for i in range(2)]
    ubbf = [esB.enter_context(nc.sbuf_tensor(f"ubbf{i}", [P, TB], BF16)) for i in range(2)]
    mtB = esB.enter_context(nc.sbuf_tensor("mtB", [P, TB], F32))
    vtB = esB.enter_context(nc.sbuf_tensor("vtB", [P, TB], F32))
    rtB = esB.enter_context(nc.sbuf_tensor("rtB", [P, TB], F32))
    tmpB = esB.enter_context(nc.sbuf_tensor("tmpB", [P, TB], F32))
    yout = [esB.enter_context(nc.sbuf_tensor(f"yout{i}", [P, TB], F32)) for i in range(2)]
    psG = [esB.enter_context(nc.psum_tensor(f"psG{i}", [P, TB], F32)) for i in range(2)]
    psU = [esB.enter_context(nc.psum_tensor(f"psU{i}", [P, TB], F32)) for i in range(2)]
    psSt = [esB.enter_context(nc.psum_tensor(f"psSt{i}", [P, TB], F32)) for i in range(2)]
    psD = [esB.enter_context(nc.psum_tensor(f"psD{i}", [P, TB], F32)) for i in range(2)]

    @blk.sync
    def _(sync):
        sync.wait_ge(S["barA2"], 4)
        sync.wait_ge(S["srs"], 1)
        for j in range(NTB):
            if j >= 1:
                sync.wait_ge(S["sbyev"], KD_ * j)
            sync.dma_start(
                out=y1t[:].rearrange("p (k t) -> p k t", k=KD_),
                in_=cc_y1.rearrange("(k p) t -> p k t", p=P)[:, :, j * TB:(j + 1) * TB],
            ).then_inc(S["sby"], 16)
            for k in range(KD_):
                if KD_ * j + k >= 4:
                    sync.wait_ge(S["sbr1"], KD_ * j + k - 3)
                sync.dma_start(
                    out=xc[k % 4][:],
                    in_=xTB[k * P:(k + 1) * P, j * TB:(j + 1) * TB],
                ).then_inc(S[f"sbx{k % 4}"], 16)
            if j == 0:
                sync.wait_ge(S["s_gu"], 1)
            for ib in range(NIB):
                if NIB * j + ib >= 2:
                    sync.wait_ge(S["sbgm"], NIB * j + ib - 1)
                sync.dma_start(
                    out=gw_t[ib % 2][:].rearrange("p (k i) -> p k i", k=KD_),
                    in_=gw_bf.rearrange("(k p) i -> p k i", p=P)[:, :, ib * P:(ib + 1) * P],
                ).then_inc(S["sbwl1" if ib % 2 else "sbwl0"], 16)
                sync.dma_start(
                    out=uw_t[ib % 2][:].rearrange("p (k i) -> p k i", k=KD_),
                    in_=uw_bf.rearrange("(k p) i -> p k i", p=P)[:, :, ib * P:(ib + 1) * P],
                ).then_inc(S["sbwl1" if ib % 2 else "sbwl0"], 16)
            if j == 0:
                sync.wait_ge(S["s_dw"], 1)
            for dc in range(KD_):
                if KD_ * j + dc >= 2:
                    sync.wait_ge(S["sbdm"], KD_ * j + dc - 1)
                sync.dma_start(
                    out=dw_t[dc % 2][:].rearrange("p (i d) -> p i d", i=NIB),
                    in_=dw_bf.rearrange("(i p) d -> p i d", p=P)[:, :, dc * P:(dc + 1) * P],
                ).then_inc(S["sbdl1" if dc % 2 else "sbdl0"], 16)

    @blk.vector
    def _(v):
        v.wait_ge(S["barA2"], 4)
        for j in range(NTB):
            v.wait_ge(S["sby"], 16 * (j + 1))
            for k in range(KD_):
                v.wait_ge(S[f"sbx{k % 4}"], 16 * (4 * j + k // 4 + 1))
                v.tensor_tensor(y1t[:, k * TB:(k + 1) * TB], y1t[:, k * TB:(k + 1) * TB],
                                xc[k % 4][:], ALU.add).then_inc(S["sbr1"], 1)
            for k in range(KD_):
                ins = v.tensor_copy(r1bf[:, k * TB:(k + 1) * TB],
                                    y1t[:, k * TB:(k + 1) * TB])
            ins.then_inc(S["sbcast"], 1)
            v.wait_ge(S["sbsm"], j + 1)
            v.tensor_scalar(mtB[:], psSt[0][:], 1.0 / D, None, ALU.mult)
            v.tensor_tensor(tmpB[:], mtB[:], mtB[:], ALU.mult)
            v.scalar_tensor_tensor(vtB[:], psSt[1][:], 1.0 / D, tmpB[:],
                                   ALU.mult, ALU.subtract).then_inc(S["sbvt"], 1)
            v.wait_ge(S["sbrt"], j + 1)
            for k in range(KD_):
                v.tensor_tensor(tmpB[:], y1t[:, k * TB:(k + 1) * TB], mtB[:],
                                ALU.subtract)
                ins = v.tensor_tensor(xn1[:, k * TB:(k + 1) * TB], tmpB[:], rtB[:],
                                      ALU.mult)
            ins.then_inc(S["sbxn"], 1)
            for ib in range(NIB):
                g = NIB * j + ib
                v.wait_ge(S["sbsg"], g + 1)
                v.tensor_scalar(ubbf[ib % 2][:], psU[ib % 2][:], us_col[:, ib:ib + 1],
                                None, ALU.mult)
                v.tensor_tensor(hh[:, ib * TB:(ib + 1) * TB], sgbf[ib % 2][:],
                                ubbf[ib % 2][:], ALU.mult).then_inc(S["sbhh"], 1)
            for dc in range(KD_):
                g = KD_ * j + dc
                v.wait_ge(S["sbdm"], g + 1)
                if 8 * j + dc // 2 >= 1:
                    v.wait_ge(S["sbst1" if dc % 2 else "sbst0"],
                              16 * (8 * j + dc // 2))
                v.scalar_tensor_tensor(yout[dc % 2][:], psD[dc % 2][:],
                                       ds_col[:, dc:dc + 1],
                                       y1t[:, dc * TB:(dc + 1) * TB],
                                       ALU.mult, ALU.add).then_inc(S["sbyev"], 1)

    @blk.tensor
    def _(t):
        t.wait_ge(S["barA2"], 4)
        for j in range(NTB):
            t.wait_ge(S["sbcast"], j + 1)
            t.wait_ge(S["ssqB"], j + 1)
            if j >= 1:
                t.wait_ge(S["sbvt"], j)
            for k in range(KD_):
                t.matmul(psSt[0][:], ones_bf[:], r1bf[:, k * TB:(k + 1) * TB],
                         start=(k == 0), stop=(k == KD_ - 1))
                ins = t.matmul(psSt[1][:], ones_bf[:], sqB[:, k * TB:(k + 1) * TB],
                               start=(k == 0), stop=(k == KD_ - 1))
            ins.then_inc(S["sbsm"], 1)
            t.wait_ge(S["sbxn"], j + 1)
            for ib in range(NIB):
                g = NIB * j + ib
                t.wait_ge(S["sbwl1" if ib % 2 else "sbwl0"],
                          32 * (22 * j + ib // 2 + 1))
                if g >= 2:
                    t.wait_ge(S["sbhh"], g - 1)
                for k in range(KD_):
                    t.matmul(psG[ib % 2][:],
                             gw_t[ib % 2][:, k * P:(k + 1) * P],
                             xn1[:, k * TB:(k + 1) * TB],
                             start=(k == 0), stop=(k == KD_ - 1))
                for k in range(KD_):
                    ins = t.matmul(psU[ib % 2][:],
                                   uw_t[ib % 2][:, k * P:(k + 1) * P],
                                   xn1[:, k * TB:(k + 1) * TB],
                                   start=(k == 0), stop=(k == KD_ - 1))
                ins.then_inc(S["sbgm"], 1)
            t.wait_ge(S["sbhh"], NIB * (j + 1))
            for dc in range(KD_):
                g = KD_ * j + dc
                t.wait_ge(S["sbdl1" if dc % 2 else "sbdl0"],
                          16 * (8 * j + dc // 2 + 1))
                if g >= 2:
                    t.wait_ge(S["sbyev"], g - 1)
                for ib in range(NIB):
                    ins = t.matmul(psD[dc % 2][:],
                                   dw_t[dc % 2][:, ib * P:(ib + 1) * P],
                                   hh[:, ib * TB:(ib + 1) * TB],
                                   start=(ib == 0), stop=(ib == NIB - 1))
                ins.then_inc(S["sbdm"], 1)

    @blk.scalar
    def _(sc):
        sc.wait_ge(S["barA2"], 4)
        for j in range(NTB):
            sc.wait_ge(S["sbvt"], j + 1)
            sc.activation(rtB[:], vtB[:], AF.Ln, bias=c_eps[:, 0:1])
            sc.activation(rtB[:], rtB[:], AF.Exp, scale=-0.5).then_inc(S["sbrt"], 1)
            for ib in range(NIB):
                g = NIB * j + ib
                sc.wait_ge(S["sbgm"], g + 1)
                if g >= 2:
                    sc.wait_ge(S["sbhh"], g - 1)
                sc.activation(sgbf[ib % 2][:], psG[ib % 2][:], AF.Silu,
                              scale=gs_col[:, ib:ib + 1]).then_inc(S["sbsg"], 1)
            for dc in range(KD_):
                g = KD_ * j + dc
                sc.wait_ge(S["sbyev"], g + 1)
                sc.dma_start(
                    out=out_y[dc * P:(dc + 1) * P, j * TB:(j + 1) * TB],
                    in_=yout[dc % 2][:],
                ).then_inc(S["sbst1" if dc % 2 else "sbst0"], 16)

    @blk.gpsimd
    def _(gp):
        gp.wait_ge(S["barA2"], 4)
        for j in range(NTB):
            gp.wait_ge(S["sbcast"], j + 1)
            for k in range(KD_):
                ins = gp.tensor_tensor(sqB[:, k * TB:(k + 1) * TB],
                                       r1bf[:, k * TB:(k + 1) * TB],
                                       r1bf[:, k * TB:(k + 1) * TB], ALU.mult)
            ins.then_inc(S["ssqB"], 1)
        gp.wait_ge(S["sbst0"], 16 * (NTB * KD_ // 2))
        gp.wait_ge(S["sbst1"], 16 * (NTB * KD_ // 2))

    esB.close()
    es.close()
    return nc


_NC_CACHE = {}


def kernel(L0_attn_x, L0_attn_log_decay, L0_attn_Wq, L0_attn_Wk, L0_attn_Wv,
           L0_attn_Wo, L0_attn_Wg1, L0_attn_Wg2, L0_attn_norm_w,
           L0_ffn_gate_w, L0_ffn_up_w, L0_ffn_down_w,
           L0_ffn_gate_s, L0_ffn_up_s, L0_ffn_down_s):
    f32 = np.float32
    x = np.asarray(L0_attn_x, f32)
    ldec = np.asarray(L0_attn_log_decay, f32)
    Wq = np.asarray(L0_attn_Wq, f32)
    Wk = np.asarray(L0_attn_Wk, f32)
    Wv = np.asarray(L0_attn_Wv, f32)
    Wo = np.asarray(L0_attn_Wo, f32)
    Wg1 = np.ascontiguousarray(np.asarray(L0_attn_Wg1, f32))
    Wg2 = np.asarray(L0_attn_Wg2, f32)
    nw = np.asarray(L0_attn_norm_w, f32)
    gw = np.asarray(L0_ffn_gate_w, f32)
    uw = np.asarray(L0_ffn_up_w, f32)
    dw = np.asarray(L0_ffn_down_w, f32)
    gsv = np.ascontiguousarray(np.asarray(L0_ffn_gate_s, f32))
    usv = np.ascontiguousarray(np.asarray(L0_ffn_up_s, f32))
    dsv = np.ascontiguousarray(np.asarray(L0_ffn_down_s, f32))

    xTb = [np.ascontiguousarray(x[b].T) for b in range(B)]
    gwT = np.ascontiguousarray(gw.T)
    uwT = np.ascontiguousarray(uw.T)
    dwT = np.ascontiguousarray(dw.T)

    idx = np.arange(P)
    diff = idx[None, :] - idx[:, None]
    DIFF = np.where(diff >= 0, diff, 1.0e9).astype(f32)
    CP1 = np.broadcast_to((idx + 1.0)[None, :], (P, P)).astype(f32)
    KREV = np.broadcast_to((P - 1.0 - idx)[:, None], (P, P)).astype(f32)
    IDT = np.eye(P, dtype=ml_dtypes.bfloat16)

    in_maps = []
    for c in range(8):
        b, hg = c // 2, c % 2
        e0, e1 = hg * EC, (hg + 1) * EC
        in_maps.append({
            "xT": xTb[b],
            "xTB": np.ascontiguousarray(xTb[b][:, hg * TOKB:(hg + 1) * TOKB]),
            "wqd": np.ascontiguousarray(Wq[:, e0:e1]),
            "wkd": np.ascontiguousarray(Wk[:, e0:e1]),
            "wvd": np.ascontiguousarray(Wv[:, e0:e1]),
            "wg1d": Wg1,
            "wg2d": np.ascontiguousarray(Wg2[:, e0:e1]),
            "wosd": np.ascontiguousarray(Wo[e0:e1, :]),
            "gwTd": gwT,
            "uwTd": uwT,
            "dwTd": dwT,
            "ld8": np.ascontiguousarray(ldec[hg * HPC:(hg + 1) * HPC]),
            "gsd": gsv,
            "usd": usv,
            "dsd": dsv,
            "nwsd": np.ascontiguousarray(nw[e0:e1]),
            "cDIFF": DIFF,
            "cCP1": CP1,
            "cKREV": KREV,
            "cIDT": IDT,
        })

    nc = build_program()
    res = run_bass_kernel_spmd(nc, in_maps, core_ids=list(range(8)))

    out = np.empty((B, N, D), f32)
    for b in range(B):
        left = np.asarray(res.results[2 * b]["out_y"])
        right = np.asarray(res.results[2 * b + 1]["out_y"])
        out[b, 0:TOKB, :] = left.T
        out[b, TOKB:N, :] = right.T
    return out


# version 08
# speedup vs baseline: 1.0697x; 1.0697x over previous
import numpy as np
import ml_dtypes
import concourse.bass as bass
import concourse.mybir as mybir
from concourse.bass_utils import run_bass_kernel_spmd
from contextlib import ExitStack

F32 = mybir.dt.float32
BF16 = mybir.dt.bfloat16
AF = mybir.ActivationFunctionType
ALU = mybir.AluOpType

P = 128
B, N, D, H = 4, 8192, 2048, 16
HG = 2
HPC = 8
EC = HPC * P
INTER = 5632
NIB = INTER // P
KD_ = D // P

TA = 256
NTA = N // TA
CPT = TA // P
NCH = N // P
TB = 512
TOKB = N // HG
NTB = TOKB // TB
TA2 = 512
NA2 = N // TA2

EPS = 1e-5
PAIRS = [[0, 1], [2, 3], [4, 5], [6, 7]]
NPG = 3 * HPC + 1 + HPC
XG = 4


def build_program():
    nc = bass.Bass("TRN2", target_bir_lowering=False, num_devices=8)

    xA = nc.dram_tensor("xA", [NTA, XG, P, 4 * TA], F32, kind="ExternalInput").ap()
    xTB = nc.dram_tensor("xTB", [D, TOKB], F32, kind="ExternalInput").ap()
    wqd = nc.dram_tensor("wqd", [P, KD_ * EC], F32, kind="ExternalInput").ap()
    wkd = nc.dram_tensor("wkd", [P, KD_ * EC], F32, kind="ExternalInput").ap()
    wvd = nc.dram_tensor("wvd", [P, KD_ * EC], F32, kind="ExternalInput").ap()
    wg1d = nc.dram_tensor("wg1d", [P, KD_ * P], F32, kind="ExternalInput").ap()
    wg2d = nc.dram_tensor("wg2d", [P, EC], F32, kind="ExternalInput").ap()
    wosd = nc.dram_tensor("wosd", [P, HPC * D], F32, kind="ExternalInput").ap()
    gwTd = nc.dram_tensor("gwTd", [NIB, P, KD_ * P], F32, kind="ExternalInput").ap()
    uwTd = nc.dram_tensor("uwTd", [NIB, P, KD_ * P], F32, kind="ExternalInput").ap()
    dwTd = nc.dram_tensor("dwTd", [KD_, P, NIB * P], F32, kind="ExternalInput").ap()
    ld8 = nc.dram_tensor("ld8", [HPC], F32, kind="ExternalInput").ap()
    gsd = nc.dram_tensor("gsd", [INTER], F32, kind="ExternalInput").ap()
    usd = nc.dram_tensor("usd", [INTER], F32, kind="ExternalInput").ap()
    dsd = nc.dram_tensor("dsd", [D], F32, kind="ExternalInput").ap()
    nwsd = nc.dram_tensor("nwsd", [EC], F32, kind="ExternalInput").ap()
    cDIFF = nc.dram_tensor("cDIFF", [P, P], F32, kind="ExternalInput").ap()
    cCP1 = nc.dram_tensor("cCP1", [P, P], F32, kind="ExternalInput").ap()
    cKREV = nc.dram_tensor("cKREV", [P, P], F32, kind="ExternalInput").ap()
    cIDT = nc.dram_tensor("cIDT", [P, P], BF16, kind="ExternalInput").ap()

    out_y = nc.dram_tensor("out_y", [D, TOKB], F32, kind="ExternalOutput").ap()

    wo_bf = nc.dram_tensor("wo_bf", [P, HPC * D], BF16, kind="Internal").ap()
    gw_bf = nc.dram_tensor("gw_bf", [NIB, P, KD_ * P], BF16, kind="Internal").ap()
    uw_bf = nc.dram_tensor("uw_bf", [NIB, P, KD_ * P], BF16, kind="Internal").ap()
    dw_bf = nc.dram_tensor("dw_bf", [KD_, P, NIB * P], BF16, kind="Internal").ap()
    onF = nc.dram_tensor("onF", [EC, N], BF16, kind="Internal").ap()
    cc_in = nc.dram_tensor("cc_in", [2 * D, TOKB], F32, kind="Internal").ap()
    cc_y1 = nc.dram_tensor("cc_y1", [D, TOKB], F32, kind="Internal").ap()

    es = ExitStack()

    semnames = [
        "sct", "sx", "sxf", "ssqx", "ssm", "svt", "srt", "sxn", "swq", "sldm", "sldb",
        "spm", "spe",
        "samm", "saev", "somm", "sogev", "sogsq", "ssmm2", "sgn", "sonm", "ssup",
        "sost0", "sost1",
        "swl0", "swl1", "gpl0", "gpl1", "gps0", "gps1", "s_wo", "s_gu", "s_dw",
        "barA", "barA2",
        "sa2w", "sa2x0", "sa2x1", "sa2mm", "sa2ev", "sa2st0", "sa2st1",
        "srs",
        "sby", "sbx0", "sbx1", "sbx2", "sbx3", "sbr1", "sbcast", "ssqB",
        "sbsm", "sbvt", "sbrt", "sbxn",
        "sbwl0", "sbwl1", "sbgm", "sbsg", "sbhh", "sbdl0", "sbdl1", "sbdm",
        "sbyev", "sbst0", "sbst1",
    ]
    S = {nm: es.enter_context(nc.semaphore(nm)) for nm in semnames}

    ones_bf = es.enter_context(nc.sbuf_tensor("ones_bf", [P, P], BF16))
    onesf_row = es.enter_context(nc.sbuf_tensor("onesf_row", [1, P], F32))
    ld_row = es.enter_context(nc.sbuf_tensor("ld_row", [1, HPC], F32))
    c_eps = es.enter_context(nc.sbuf_tensor("c_eps", [P, 1], F32))
    c_ln05 = es.enter_context(nc.sbuf_tensor("c_ln05", [P, 1], F32))
    LDB = es.enter_context(nc.sbuf_tensor("LDB", [P, HPC], F32))
    MT = es.enter_context(nc.sbuf_tensor("MT", [P, EC], BF16))
    QD = es.enter_context(nc.sbuf_tensor("QD", [P, EC], BF16))
    KDt = es.enter_context(nc.sbuf_tensor("KDt", [P, EC], BF16))
    CDt = es.enter_context(nc.sbuf_tensor("CDt", [P, EC], F32))
    idt = es.enter_context(nc.sbuf_tensor("idt", [P, P], BF16))
    gs_col = es.enter_context(nc.sbuf_tensor("gs_col", [P, NIB], F32))
    us_col = es.enter_context(nc.sbuf_tensor("us_col", [P, NIB], F32))
    ds_col = es.enter_context(nc.sbuf_tensor("ds_col", [P, KD_], F32))
    nw_col = es.enter_context(nc.sbuf_tensor("nw_col", [P, HPC], F32))
    c128t = es.enter_context(nc.sbuf_tensor("c128t", [P, P], F32))
    dsc = es.enter_context(nc.sbuf_tensor("dsc", [P, 3 * P], F32))
    WPW = 1024
    stw = [es.enter_context(nc.sbuf_tensor(f"stw{i}", [P, WPW], F32)) for i in range(2)]
    stwb = [es.enter_context(nc.sbuf_tensor(f"stwb{i}", [P, WPW], BF16)) for i in range(2)]

    prep_jobs = []

    def add_prep_flat(src2d, dst2d, cols, nwcol_of=None):
        c0 = 0
        while c0 < cols:
            w = min(WPW, cols - c0)
            prep_jobs.append((
                src2d[:, c0:c0 + w], dst2d[:, c0:c0 + w],
                (nwcol_of(c0) if nwcol_of is not None else None), w))
            c0 += w

    add_prep_flat(wosd, wo_bf, HPC * D, nwcol_of=lambda c0: nw_col[:, c0 // D:c0 // D + 1])
    n_wo_jobs = len(prep_jobs)
    for ib in range(NIB):
        add_prep_flat(gwTd[ib], gw_bf[ib], KD_ * P)
    for ib in range(NIB):
        add_prep_flat(uwTd[ib], uw_bf[ib], KD_ * P)
    n_gu_jobs = len(prep_jobs)
    for dc in range(KD_):
        add_prep_flat(dwTd[dc], dw_bf[dc], NIB * P)
    n_all_jobs = len(prep_jobs)
    n_wslab = 3 * KD_ + 2 + 1

    esA = ExitStack()
    wq_bf = esA.enter_context(nc.sbuf_tensor("wq_bf", [P, KD_ * EC], BF16))
    wk_bf = esA.enter_context(nc.sbuf_tensor("wk_bf", [P, KD_ * EC], BF16))
    wv_bf = esA.enter_context(nc.sbuf_tensor("wv_bf", [P, KD_ * EC], BF16))
    wg1_bf = esA.enter_context(nc.sbuf_tensor("wg1_bf", [P, KD_ * P], BF16))
    wg2_bf = esA.enter_context(nc.sbuf_tensor("wg2_bf", [P, EC], BF16))
    stag = esA.enter_context(nc.sbuf_tensor("stag", [P, 4 * TA], F32))
    xbf = esA.enter_context(nc.sbuf_tensor("xbf", [P, KD_ * TA], BF16))
    sqb = esA.enter_context(nc.sbuf_tensor("sqb", [P, KD_ * TA], BF16))
    xn2 = [esA.enter_context(nc.sbuf_tensor(f"xn{i}", [P, KD_ * TA], BF16)) for i in range(2)]
    qbf = esA.enter_context(nc.sbuf_tensor("qbf", [P, HPC * TA], BF16))
    kbf = esA.enter_context(nc.sbuf_tensor("kbf", [P, HPC * TA], BF16))
    vbf = esA.enter_context(nc.sbuf_tensor("vbf", [P, HPC * TA], BF16))
    g1bf = esA.enter_context(nc.sbuf_tensor("g1bf", [P, TA], BF16))
    tgbf = esA.enter_context(nc.sbuf_tensor("tgbf", [P, HPC * TA], BF16))
    mt_s = esA.enter_context(nc.sbuf_tensor("mt_s", [P, TA], F32))
    vt_s = esA.enter_context(nc.sbuf_tensor("vt_s", [P, TA], F32))
    rt_s = esA.enter_context(nc.sbuf_tensor("rt_s", [P, TA], F32))
    tmp_s = esA.enter_context(nc.sbuf_tensor("tmp_s", [P, TA], F32))
    scT = esA.enter_context(nc.sbuf_tensor("scT", [P, EC], BF16))
    vt_bf = esA.enter_context(nc.sbuf_tensor("vt_bf", [P, EC], BF16))
    kt_bf = esA.enter_context(nc.sbuf_tensor("kt_bf", [P, EC], BF16))
    qp_bf = esA.enter_context(nc.sbuf_tensor("qp_bf", [P, EC], BF16))
    onormbf = [esA.enter_context(nc.sbuf_tensor(f"onormbf{i}", [P, EC], BF16))
               for i in range(2)]
    St = esA.enter_context(nc.sbuf_tensor("St", [P, EC], F32))
    S_bf = esA.enter_context(nc.sbuf_tensor("S_bf", [P, EC], BF16))
    ogbf = vt_bf
    ogsq = kt_bf

    psA = esA.enter_context(nc.psum_tensor("psA", [P, EC], F32))
    psBt = esA.enter_context(nc.psum_tensor("psBt", [P, 2 * EC], BF16))
    psC = esA.enter_context(nc.psum_tensor("psC", [P, EC], F32))
    psP = [esA.enter_context(nc.psum_tensor(f"psP{i}", [P, 512], F32)) for i in range(2)]

    def hc(t):
        return t[:].rearrange("p (h c) -> p h c", h=HPC)

    def xsrc(i, g):
        return xA[i, g]

    prep_state = {"j": 0, "issued": 0}

    def prep_slice(gp, maxjobs):
        def issue_load():
            jx = prep_state["issued"]
            if jx < n_all_jobs:
                psrc, pdst, nwf, w = prep_jobs[jx]
                gp.dma_start(out=stw[jx % 2][:, 0:w], in_=psrc).then_inc(
                    S["gpl1" if jx % 2 else "gpl0"], 16)
                prep_state["issued"] = jx + 1

        for _ in range(maxjobs):
            jx = prep_state["j"]
            if jx >= n_all_jobs:
                return
            while prep_state["issued"] <= jx + 1 and prep_state["issued"] < n_all_jobs:
                issue_load()
            psrc, pdst, nwf, w = prep_jobs[jx]
            gp.wait_ge(S["gpl1" if jx % 2 else "gpl0"], 16 * (jx // 2 + 1))
            if jx >= 2:
                gp.wait_ge(S["gps1" if jx % 2 else "gps0"], 16 * (jx // 2))
            if nwf is not None:
                gp.tensor_scalar(stwb[jx % 2][:, 0:w], stw[jx % 2][:, 0:w], nwf,
                                 None, ALU.mult)
            else:
                gp.tensor_copy(stwb[jx % 2][:, 0:w], stw[jx % 2][:, 0:w])
            gp.dma_start(out=pdst, in_=stwb[jx % 2][:, 0:w]).then_inc(
                S["gps1" if jx % 2 else "gps0"], 16)
            prep_state["j"] = jx + 1
            for nj, sem in ((n_wo_jobs, "s_wo"), (n_gu_jobs, "s_gu"), (n_all_jobs, "s_dw")):
                if prep_state["j"] == nj:
                    gp.wait_ge(S["gps0"], 16 * ((nj + 1) // 2))
                    gp.wait_ge(S["gps1"], 16 * (nj // 2))
                    gp.tensor_copy(c128t[:, 0:1], c128t[:, 0:1]).then_inc(S[sem], 1)

    blk = es.enter_context(nc.Block())

    NCONST = 9

    def sx_tgt(i, g):
        return 16 * (XG * i + g + 1)

    @blk.sync
    def _(sync):
        sync.dma_start(out=dsc[:, 0:P], in_=cDIFF).then_inc(S["sct"], 16)
        sync.dma_start(out=dsc[:, P:2 * P], in_=cCP1).then_inc(S["sct"], 16)
        sync.dma_start(out=dsc[:, 2 * P:3 * P], in_=cKREV).then_inc(S["sct"], 16)
        sync.dma_start(out=idt[:], in_=cIDT).then_inc(S["sct"], 16)
        sync.dma_start(out=ld_row[:], in_=ld8.rearrange("(one h) -> one h", one=1)
                       ).then_inc(S["sct"], 16)
        sync.dma_start(out=gs_col[:], in_=gsd.rearrange("(i p) -> p i", p=P),
                       allow_slow_non_contiguous=True).then_inc(S["sct"], 16)
        sync.dma_start(out=us_col[:], in_=usd.rearrange("(i p) -> p i", p=P),
                       allow_slow_non_contiguous=True).then_inc(S["sct"], 16)
        sync.dma_start(out=ds_col[:], in_=dsd.rearrange("(i p) -> p i", p=P),
                       allow_slow_non_contiguous=True).then_inc(S["sct"], 16)
        sync.dma_start(out=nw_col[:], in_=nwsd.rearrange("(i p) -> p i", p=P),
                       allow_slow_non_contiguous=True).then_inc(S["sct"], 16)
        for g in range(XG):
            if XG * 0 + g >= 1:
                sync.wait_ge(S["sxf"], XG * 0 + g)
            sync.dma_start(out=stag[:], in_=xsrc(0, g)).then_inc(S["sx"], 16)
        wslabs = []
        for w in (wqd, wkd, wvd):
            for r in range(KD_):
                wslabs.append((w[:, r * EC:(r + 1) * EC], EC))
        for r in range(2):
            wslabs.append((wg1d[:, r * 1024:(r + 1) * 1024], 1024))
        wslabs.append((wg2d[:, :], EC))
        for j, (src, cols) in enumerate(wslabs):
            if j >= 2:
                sync.wait_ge(S["swq"], j - 1)
            sync.dma_start(out=stw[j % 2][:, 0:cols], in_=src).then_inc(
                S["swl1" if j % 2 else "swl0"], 16)
        for i in range(1, NTA):
            for g in range(XG):
                sync.wait_ge(S["sxf"], XG * i + g)
                sync.dma_start(out=stag[:], in_=xsrc(i, g)).then_inc(S["sx"], 16)

    @blk.vector
    def _(v):
        v.memset(ones_bf[:], 1.0)
        v.memset(onesf_row[:], 1.0)
        v.memset(c_eps[:], EPS)
        v.memset(c_ln05[:], float(np.log(0.5)))
        v.memset(c128t[:], float(P))
        v.memset(St[:], 0.0)
        v.memset(S_bf[:], 0.0).then_inc(S["ssup"], 1)
        for g in range(XG):
            v.wait_ge(S["sx"], sx_tgt(0, g))
            for k in range(4):
                ki = 4 * g + k
                ins = v.tensor_copy(xbf[:, ki * TA:(ki + 1) * TA],
                                    stag[:, k * TA:(k + 1) * TA])
            ins.then_inc(S["sxf"], 1)
        wdst = []
        for wt in (wq_bf, wk_bf, wv_bf):
            for r in range(KD_):
                wdst.append((wt[:, r * EC:(r + 1) * EC], EC))
        for r in range(2):
            wdst.append((wg1_bf[:, r * 1024:(r + 1) * 1024], 1024))
        wdst.append((wg2_bf[:, 0:EC], EC))
        for j, (dst, cols) in enumerate(wdst):
            v.wait_ge(S["swl1" if j % 2 else "swl0"], 16 * (j // 2 + 1))
            v.tensor_copy(dst, stw[j % 2][:, 0:cols]).then_inc(S["swq"], 1)
        v.wait_ge(S["ssm"], 1)
        v.tensor_scalar(mt_s[:], psP[0][:, 0:TA], 1.0 / D, None, ALU.mult)
        v.tensor_tensor(tmp_s[:], mt_s[:], mt_s[:], ALU.mult)
        v.scalar_tensor_tensor(vt_s[:], psP[1][:, 0:TA], 1.0 / D, tmp_s[:],
                               ALU.mult, ALU.subtract).then_inc(S["svt"], 1)
        v.wait_ge(S["srt"], 1)
        for k in range(KD_):
            v.tensor_tensor(tmp_s[:], xbf[:, k * TA:(k + 1) * TA], mt_s[:], ALU.subtract)
            ins = v.tensor_tensor(xn2[0][:, k * TA:(k + 1) * TA], tmp_s[:], rt_s[:],
                                  ALU.mult)
        ins.then_inc(S["sxn"], 1)

        for i in range(NTA):
            if i < NTA - 1:
                v.wait_ge(S["ssm"], i + 1)
                for g in range(XG):
                    v.wait_ge(S["sx"], sx_tgt(i + 1, g))
                    for k in range(4):
                        ki = 4 * g + k
                        ins = v.tensor_copy(xbf[:, ki * TA:(ki + 1) * TA],
                                            stag[:, k * TA:(k + 1) * TA])
                    ins.then_inc(S["sxf"], 1)
            for cc in range(CPT):
                n = CPT * i + cc
                v.wait_ge(S["samm"], n + 1)
                v.tensor_tensor(scT[:], psA[:], MT[:], ALU.mult)
                v.tensor_copy(vt_bf[:], psBt[:, 0:EC])
                v.tensor_tensor(kt_bf[:], psBt[:, EC:2 * EC], KDt[:], ALU.mult)
                v.tensor_tensor(
                    hc(qp_bf),
                    qbf[:].rearrange("p (h t) -> p h t", h=HPC)[:, :, cc * P:(cc + 1) * P],
                    hc(QD), ALU.mult).then_inc(S["saev"], 1)
                v.wait_ge(S["somm"], n + 1)
                v.scalar_tensor_tensor(
                    hc(ogbf),
                    tgbf[:].rearrange("p (h t) -> p h t", h=HPC)[:, :, cc * P:(cc + 1) * P],
                    1.0, hc(psA), ALU.add, ALU.mult).then_inc(S["sogev"], 1)
                v.tensor_tensor(St[:], St[:], CDt[:], ALU.mult)
                v.tensor_tensor(St[:], St[:], psC[:], ALU.add)
                v.tensor_copy(S_bf[:], St[:]).then_inc(S["ssup"], 1)
                v.wait_ge(S["sgn"], n + 1)
                if n >= 2:
                    v.wait_ge(S["sost1" if n % 2 else "sost0"], 16 * (n // 2))
                v.tensor_tensor(onormbf[n % 2][:], ogbf[:], psC[:],
                                ALU.mult).then_inc(S["sonm"], 1)
            if i < NTA - 1:
                v.wait_ge(S["ssm"], i + 2)
                v.tensor_scalar(mt_s[:], psP[0][:, 0:TA], 1.0 / D, None, ALU.mult)
                v.tensor_tensor(tmp_s[:], mt_s[:], mt_s[:], ALU.mult)
                v.scalar_tensor_tensor(vt_s[:], psP[1][:, 0:TA], 1.0 / D, tmp_s[:],
                                       ALU.mult, ALU.subtract).then_inc(S["svt"], 1)
                v.wait_ge(S["srt"], i + 2)
                for k in range(KD_):
                    v.tensor_tensor(tmp_s[:], xbf[:, k * TA:(k + 1) * TA], mt_s[:],
                                    ALU.subtract)
                    ins = v.tensor_tensor(xn2[(i + 1) % 2][:, k * TA:(k + 1) * TA],
                                          tmp_s[:], rt_s[:], ALU.mult)
                ins.then_inc(S["sxn"], 1)
        v.wait_ge(S["sost0"], 16 * (NCH // 2))
        v.wait_ge(S["sost1"], 16 * (NCH // 2))
        v.tensor_copy(mt_s[:, 0:1], mt_s[:, 0:1]).then_inc(S["barA"], 1)

    @blk.tensor
    def _(t):
        t.wait_ge(S["sct"], 16 * 9)
        t.wait_ge(S["ssup"], 1)
        t.matmul(psP[0][:, 0:HPC], onesf_row[:], ld_row[:],
                 start=True, stop=True).then_inc(S["sldm"], 1)
        t.wait_ge(S["sldb"], 1)
        t.wait_ge(S["sxf"], XG)
        t.wait_ge(S["ssqx"], 1)
        for k in range(KD_):
            t.matmul(psP[0][:, 0:TA], ones_bf[:], xbf[:, k * TA:(k + 1) * TA],
                     start=(k == 0), stop=(k == KD_ - 1))
            ins = t.matmul(psP[1][:, 0:TA], ones_bf[:], sqb[:, k * TA:(k + 1) * TA],
                           start=(k == 0), stop=(k == KD_ - 1))
        ins.then_inc(S["ssm"], 1)
        t.wait_ge(S["swq"], n_wslab)

        for i in range(NTA):
            t.wait_ge(S["sxn"], i + 1)
            g = 0
            for wt in (wq_bf, wk_bf, wv_bf):
                for e in range(HPC):
                    if 33 * i + g - 1 >= 1:
                        t.wait_ge(S["spe"], 33 * i + g - 1)
                    if g < 2:
                        t.wait_ge(S["svt"], i + 1)
                    for k in range(KD_):
                        ins = t.matmul(psP[g % 2][:, 0:TA],
                                       wt[:, k * EC + e * P:k * EC + (e + 1) * P],
                                       xn2[i % 2][:, k * TA:(k + 1) * TA],
                                       start=(k == 0), stop=(k == KD_ - 1))
                    ins.then_inc(S["spm"], 1)
                    g += 1
            t.wait_ge(S["spe"], 33 * i + g - 1)
            for k in range(KD_):
                ins = t.matmul(psP[g % 2][:, 0:TA],
                               wg1_bf[:, k * P:(k + 1) * P],
                               xn2[i % 2][:, k * TA:(k + 1) * TA],
                               start=(k == 0), stop=(k == KD_ - 1))
            ins.then_inc(S["spm"], 1)
            g += 1
            for e in range(HPC):
                t.wait_ge(S["spe"], max(33 * i + g - 1, 33 * i + 25))
                t.matmul(psP[g % 2][:, 0:TA],
                         wg2_bf[:, e * P:(e + 1) * P], g1bf[:],
                         start=True, stop=True).then_inc(S["spm"], 1)
                g += 1

            for cc in range(CPT):
                n = CPT * i + cc
                if cc == 0:
                    t.wait_ge(S["spe"], 33 * (i + 1))
                if n >= 1:
                    t.wait_ge(S["sgn"], n)
                for h in range(HPC):
                    t.matmul(psA[:, h * P:(h + 1) * P],
                             kbf[:, h * TA + cc * P:h * TA + (cc + 1) * P],
                             qbf[:, h * TA + cc * P:h * TA + (cc + 1) * P],
                             start=True, stop=True)
                for h in range(HPC):
                    t.transpose(psBt[:, h * P:(h + 1) * P],
                                vbf[:, h * TA + cc * P:h * TA + (cc + 1) * P], idt[:])
                for h in range(HPC):
                    ins = t.transpose(psBt[:, EC + h * P:EC + (h + 1) * P],
                                      kbf[:, h * TA + cc * P:h * TA + (cc + 1) * P],
                                      idt[:])
                ins.then_inc(S["samm"], 1)
                t.wait_ge(S["saev"], n + 1)
                t.wait_ge(S["ssup"], n + 1)
                if n >= 1:
                    t.wait_ge(S["sonm"], n)
                for h in range(HPC):
                    t.matmul(psA[:, h * P:(h + 1) * P],
                             vt_bf[:, h * P:(h + 1) * P], scT[:, h * P:(h + 1) * P],
                             start=True, stop=False)
                    t.matmul(psA[:, h * P:(h + 1) * P],
                             S_bf[:, h * P:(h + 1) * P], qp_bf[:, h * P:(h + 1) * P],
                             start=False, stop=True)
                for h in range(HPC):
                    ins = t.matmul(psC[:, h * P:(h + 1) * P],
                                   kt_bf[:, h * P:(h + 1) * P],
                                   vt_bf[:, h * P:(h + 1) * P],
                                   start=True, stop=True)
                ins.then_inc(S["somm"], 1)
                t.wait_ge(S["sogsq"], n + 1)
                t.matmul(psA[:, 0:512], ones_bf[:], ogsq[:, 0:512], start=True, stop=True)
                t.matmul(psA[:, 512:EC], ones_bf[:], ogsq[:, 512:EC],
                         start=True, stop=True).then_inc(S["ssmm2"], 1)

            if i < NTA - 1:
                t.wait_ge(S["sxf"], XG * (i + 2))
                t.wait_ge(S["ssqx"], i + 2)
                t.wait_ge(S["svt"], i + 1)
                t.wait_ge(S["spe"], 33 * (i + 1))
                for k in range(KD_):
                    t.matmul(psP[0][:, 0:TA], ones_bf[:], xbf[:, k * TA:(k + 1) * TA],
                             start=(k == 0), stop=(k == KD_ - 1))
                    ins = t.matmul(psP[1][:, 0:TA], ones_bf[:],
                                   sqb[:, k * TA:(k + 1) * TA],
                                   start=(k == 0), stop=(k == KD_ - 1))
                ins.then_inc(S["ssm"], 1)
        t.wait_ge(S["ssmm2"], NCH)
        t.matmul(psP[0][:, 0:P], ones_bf[:], ones_bf[:],
                 start=True, stop=True).then_inc(S["barA"], 1)

    @blk.scalar
    def _(sc):
        sc.wait_ge(S["sldm"], 1)
        sc.copy(LDB[:], psP[0][:, 0:HPC]).then_inc(S["sldb"], 1)
        sc.wait_ge(S["sldb"], 1)
        DIFF = dsc[:, 0:P]
        CP1 = dsc[:, P:2 * P]
        KREV = dsc[:, 2 * P:3 * P]
        for h in range(HPC):
            sc.activation(MT[:, h * P:(h + 1) * P], DIFF, AF.Exp, scale=LDB[:, h:h + 1])
            sc.activation(QD[:, h * P:(h + 1) * P], CP1, AF.Exp, scale=LDB[:, h:h + 1])
            sc.activation(KDt[:, h * P:(h + 1) * P], KREV, AF.Exp, scale=LDB[:, h:h + 1])
            sc.activation(CDt[:, h * P:(h + 1) * P], c128t[:], AF.Exp,
                          scale=LDB[:, h:h + 1])
        sc.wait_ge(S["svt"], 1)
        sc.activation(rt_s[:], vt_s[:], AF.Ln, bias=c_eps[:, 0:1])
        sc.activation(rt_s[:], rt_s[:], AF.Exp, scale=-0.5).then_inc(S["srt"], 1)

        for i in range(NTA):
            g = 0
            for dst in (qbf, kbf, vbf):
                for e in range(HPC):
                    sc.wait_ge(S["spm"], 33 * i + g + 1)
                    sc.activation(dst[:, e * TA:(e + 1) * TA], psP[g % 2][:, 0:TA],
                                  AF.Silu).then_inc(S["spe"], 1)
                    g += 1
            sc.wait_ge(S["spm"], 33 * i + g + 1)
            sc.copy(g1bf[:], psP[g % 2][:, 0:TA]).then_inc(S["spe"], 1)
            g += 1
            for e in range(HPC):
                sc.wait_ge(S["spm"], 33 * i + g + 1)
                sc.activation(tgbf[:, e * TA:(e + 1) * TA], psP[g % 2][:, 0:TA],
                              AF.Tanh, scale=0.5).then_inc(S["spe"], 1)
                g += 1
            for cc in range(CPT):
                n = CPT * i + cc
                sc.wait_ge(S["ssmm2"], n + 1)
                sc.wait_ge(S["ssup"], n + 2)
                sc.activation(psC[:], psA[:], AF.Ln, bias=c_eps[:, 0:1], scale=1.0 / 512.0)
                sc.activation(psC[:], psC[:], AF.Exp, bias=c_ln05[:, 0:1],
                              scale=-0.5).then_inc(S["sgn"], 1)
                sc.wait_ge(S["sonm"], n + 1)
                sc.dma_start(
                    out=onF.rearrange("(h e) t -> e h t", e=P)[:, :, n * P:(n + 1) * P],
                    in_=onormbf[n % 2][:].rearrange("e (h c) -> e h c", h=HPC),
                ).then_inc(S["sost1" if n % 2 else "sost0"], 16)
            if i < NTA - 1:
                sc.wait_ge(S["svt"], i + 2)
                sc.activation(rt_s[:], vt_s[:], AF.Ln, bias=c_eps[:, 0:1])
                sc.activation(rt_s[:], rt_s[:], AF.Exp, scale=-0.5).then_inc(S["srt"], 1)
        sc.wait_ge(S["sost0"], 16 * (NCH // 2))
        sc.wait_ge(S["sost1"], 16 * (NCH // 2))
        sc.copy(LDB[:, 0:1], LDB[:, 0:1]).then_inc(S["barA"], 1)

    @blk.gpsimd
    def _(gp):
        gp.wait_ge(S["sxf"], XG)
        for k in range(KD_):
            ins = gp.tensor_tensor(sqb[:, k * TA:(k + 1) * TA],
                                   xbf[:, k * TA:(k + 1) * TA],
                                   xbf[:, k * TA:(k + 1) * TA], ALU.mult)
        ins.then_inc(S["ssqx"], 1)

        gp.wait_ge(S["swq"], n_wslab)
        for i in range(NTA):
            if i < NTA - 1:
                gp.wait_ge(S["sxf"], XG * (i + 2))
                for k in range(KD_):
                    ins = gp.tensor_tensor(sqb[:, k * TA:(k + 1) * TA],
                                           xbf[:, k * TA:(k + 1) * TA],
                                           xbf[:, k * TA:(k + 1) * TA], ALU.mult)
                ins.then_inc(S["ssqx"], 1)
            for cc in range(CPT):
                n = CPT * i + cc
                gp.wait_ge(S["sogev"], n + 1)
                gp.tensor_tensor(ogsq[:], ogbf[:], ogbf[:], ALU.mult).then_inc(
                    S["sogsq"], 1)
            prep_slice(gp, 4)
        prep_slice(gp, n_all_jobs)
        gp.tensor_copy(c128t[:, 0:1], c128t[:, 0:1]).then_inc(S["barA"], 1)

    esA.close()
    esA2 = ExitStack()
    wo2_bf = esA2.enter_context(nc.sbuf_tensor("wo2_bf", [P, HPC * D], BF16))
    on_t = [esA2.enter_context(nc.sbuf_tensor(f"on_t{i}", [P, HPC * TA2], BF16))
            for i in range(2)]
    y1ev = [esA2.enter_context(nc.sbuf_tensor(f"y1ev{i}", [P, TA2], F32))
            for i in range(2)]
    psY2 = [esA2.enter_context(nc.psum_tensor(f"psY2{i}", [P, TA2], F32))
            for i in range(2)]

    @blk.sync
    def _(sync):
        sync.wait_ge(S["barA"], 4)
        sync.wait_ge(S["s_wo"], 1)
        sync.dma_start(out=wo2_bf[:], in_=wo_bf).then_inc(S["sa2w"], 16)
        for j in range(NA2):
            if j >= 2:
                sync.wait_ge(S["sa2mm"], KD_ * (j - 1))
            sync.dma_start(
                out=on_t[j % 2][:].rearrange("p (k t) -> p k t", k=HPC),
                in_=onF.rearrange("(k p) t -> p k t", p=P)[:, :, j * TA2:(j + 1) * TA2],
            ).then_inc(S["sa2x1" if j % 2 else "sa2x0"], 16)

    @blk.tensor
    def _(t):
        t.wait_ge(S["barA"], 4)
        t.wait_ge(S["sa2w"], 16)
        for j in range(NA2):
            t.wait_ge(S["sa2x1" if j % 2 else "sa2x0"], 16 * (j // 2 + 1))
            for e in range(KD_):
                g = KD_ * j + e
                if g >= 2:
                    t.wait_ge(S["sa2ev"], g - 1)
                for k in range(HPC):
                    ins = t.matmul(psY2[e % 2][:],
                                   wo2_bf[:, k * D + e * P:k * D + (e + 1) * P],
                                   on_t[j % 2][:, k * TA2:(k + 1) * TA2],
                                   start=(k == 0), stop=(k == HPC - 1))
                ins.then_inc(S["sa2mm"], 1)

    @blk.vector
    def _(v):
        v.wait_ge(S["barA"], 4)
        for j in range(NA2):
            for e in range(KD_):
                g = KD_ * j + e
                v.wait_ge(S["sa2mm"], g + 1)
                if g >= 2:
                    v.wait_ge(S["sa2st1" if g % 2 else "sa2st0"], 16 * (g // 2))
                v.tensor_copy(y1ev[e % 2][:], psY2[e % 2][:]).then_inc(S["sa2ev"], 1)

    @blk.scalar
    def _(sc):
        sc.wait_ge(S["barA"], 4)
        for j in range(NA2):
            hrow = (j * TA2) // TOKB
            tl = (j * TA2) % TOKB
            for e in range(KD_):
                g = KD_ * j + e
                sc.wait_ge(S["sa2ev"], g + 1)
                sc.dma_start(
                    out=cc_in[hrow * D + e * P:hrow * D + (e + 1) * P, tl:tl + TA2],
                    in_=y1ev[e % 2][:],
                ).then_inc(S["sa2st1" if g % 2 else "sa2st0"], 16)
        sc.wait_ge(S["sa2st0"], 16 * (NA2 * KD_ // 2))
        sc.wait_ge(S["sa2st1"], 16 * (NA2 * KD_ // 2))
        sc.copy(LDB[:, 0:1], LDB[:, 0:1]).then_inc(S["barA2"], 1)

    @blk.gpsimd
    def _(gp):
        gp.wait_ge(S["sa2st0"], 16 * (NA2 * KD_ // 2))
        gp.wait_ge(S["sa2st1"], 16 * (NA2 * KD_ // 2))
        gp.collective_compute(
            "ReduceScatter", ALU.add, replica_groups=PAIRS,
            ins=[cc_in], outs=[cc_y1],
        ).then_inc(S["srs"], 1)
        gp.wait_ge(S["srs"], 1)
        gp.tensor_copy(c128t[:, 0:1], c128t[:, 0:1]).then_inc(S["barA2"], 1)

    @blk.tensor
    def _(t):
        t.wait_ge(S["sa2mm"], NA2 * KD_)
        t.matmul(psY2[0][:, 0:P], ones_bf[:], ones_bf[:],
                 start=True, stop=True).then_inc(S["barA2"], 1)

    @blk.vector
    def _(v):
        v.wait_ge(S["sa2ev"], NA2 * KD_)
        v.tensor_copy(c128t[:, 0:1], c128t[:, 0:1]).then_inc(S["barA2"], 1)

    esA2.close()
    esB = ExitStack()
    y1t = esB.enter_context(nc.sbuf_tensor("y1t", [P, KD_ * TB], F32))
    xc = [esB.enter_context(nc.sbuf_tensor(f"xc{i}", [P, TB], F32)) for i in range(4)]
    r1bf = esB.enter_context(nc.sbuf_tensor("r1bf", [P, KD_ * TB], BF16))
    sqB = esB.enter_context(nc.sbuf_tensor("sqB", [P, KD_ * TB], BF16))
    xn1 = r1bf
    hh = esB.enter_context(nc.sbuf_tensor("hh", [P, NIB * TB], BF16))
    gw_t = [esB.enter_context(nc.sbuf_tensor(f"gw_t{i}", [P, KD_ * P], BF16))
            for i in range(2)]
    uw_t = [esB.enter_context(nc.sbuf_tensor(f"uw_t{i}", [P, KD_ * P], BF16))
            for i in range(2)]
    dw_t = [esB.enter_context(nc.sbuf_tensor(f"dw_t{i}", [P, NIB * P], BF16))
            for i in range(2)]
    sgbf = [esB.enter_context(nc.sbuf_tensor(f"sgbf{i}", [P, TB], BF16)) for i in range(2)]
    ubbf = [esB.enter_context(nc.sbuf_tensor(f"ubbf{i}", [P, TB], BF16)) for i in range(2)]
    mtB = esB.enter_context(nc.sbuf_tensor("mtB", [P, TB], F32))
    vtB = esB.enter_context(nc.sbuf_tensor("vtB", [P, TB], F32))
    rtB = esB.enter_context(nc.sbuf_tensor("rtB", [P, TB], F32))
    tmpB = esB.enter_context(nc.sbuf_tensor("tmpB", [P, TB], F32))
    yout = [esB.enter_context(nc.sbuf_tensor(f"yout{i}", [P, TB], F32)) for i in range(2)]
    psG = [esB.enter_context(nc.psum_tensor(f"psG{i}", [P, TB], F32)) for i in range(2)]
    psU = [esB.enter_context(nc.psum_tensor(f"psU{i}", [P, TB], F32)) for i in range(2)]
    psSt = [esB.enter_context(nc.psum_tensor(f"psSt{i}", [P, TB], F32)) for i in range(2)]
    psD = [esB.enter_context(nc.psum_tensor(f"psD{i}", [P, TB], F32)) for i in range(2)]

    @blk.sync
    def _(sync):
        sync.wait_ge(S["barA2"], 4)
        sync.wait_ge(S["srs"], 1)
        for j in range(NTB):
            if j >= 1:
                sync.wait_ge(S["sbyev"], KD_ * j)
            sync.dma_start(
                out=y1t[:].rearrange("p (k t) -> p k t", k=KD_),
                in_=cc_y1.rearrange("(k p) t -> p k t", p=P)[:, :, j * TB:(j + 1) * TB],
            ).then_inc(S["sby"], 16)
            for k in range(KD_):
                if KD_ * j + k >= 4:
                    sync.wait_ge(S["sbr1"], KD_ * j + k - 3)
                sync.dma_start(
                    out=xc[k % 4][:],
                    in_=xTB[k * P:(k + 1) * P, j * TB:(j + 1) * TB],
                ).then_inc(S[f"sbx{k % 4}"], 16)
            if j == 0:
                sync.wait_ge(S["s_gu"], 1)
            for ib in range(NIB):
                if NIB * j + ib >= 2:
                    sync.wait_ge(S["sbgm"], NIB * j + ib - 1)
                sync.dma_start(out=gw_t[ib % 2][:], in_=gw_bf[ib]
                               ).then_inc(S["sbwl1" if ib % 2 else "sbwl0"], 16)
                sync.dma_start(out=uw_t[ib % 2][:], in_=uw_bf[ib]
                               ).then_inc(S["sbwl1" if ib % 2 else "sbwl0"], 16)
            if j == 0:
                sync.wait_ge(S["s_dw"], 1)
            for dc in range(KD_):
                if KD_ * j + dc >= 2:
                    sync.wait_ge(S["sbdm"], KD_ * j + dc - 1)
                sync.dma_start(out=dw_t[dc % 2][:], in_=dw_bf[dc]
                               ).then_inc(S["sbdl1" if dc % 2 else "sbdl0"], 16)

    @blk.vector
    def _(v):
        v.wait_ge(S["barA2"], 4)
        for j in range(NTB):
            v.wait_ge(S["sby"], 16 * (j + 1))
            for k in range(KD_):
                v.wait_ge(S[f"sbx{k % 4}"], 16 * (4 * j + k // 4 + 1))
                v.tensor_tensor(y1t[:, k * TB:(k + 1) * TB], y1t[:, k * TB:(k + 1) * TB],
                                xc[k % 4][:], ALU.add).then_inc(S["sbr1"], 1)
            for k in range(KD_):
                ins = v.tensor_copy(r1bf[:, k * TB:(k + 1) * TB],
                                    y1t[:, k * TB:(k + 1) * TB])
            ins.then_inc(S["sbcast"], 1)
            v.wait_ge(S["sbsm"], j + 1)
            v.tensor_scalar(mtB[:], psSt[0][:], 1.0 / D, None, ALU.mult)
            v.tensor_tensor(tmpB[:], mtB[:], mtB[:], ALU.mult)
            v.scalar_tensor_tensor(vtB[:], psSt[1][:], 1.0 / D, tmpB[:],
                                   ALU.mult, ALU.subtract).then_inc(S["sbvt"], 1)
            v.wait_ge(S["sbrt"], j + 1)
            for k in range(KD_):
                v.tensor_tensor(tmpB[:], y1t[:, k * TB:(k + 1) * TB], mtB[:],
                                ALU.subtract)
                ins = v.tensor_tensor(xn1[:, k * TB:(k + 1) * TB], tmpB[:], rtB[:],
                                      ALU.mult)
            ins.then_inc(S["sbxn"], 1)
            for ib in range(NIB):
                g = NIB * j + ib
                v.wait_ge(S["sbsg"], g + 1)
                v.tensor_scalar(ubbf[ib % 2][:], psU[ib % 2][:], us_col[:, ib:ib + 1],
                                None, ALU.mult)
                v.tensor_tensor(hh[:, ib * TB:(ib + 1) * TB], sgbf[ib % 2][:],
                                ubbf[ib % 2][:], ALU.mult).then_inc(S["sbhh"], 1)
            for dc in range(KD_):
                g = KD_ * j + dc
                v.wait_ge(S["sbdm"], g + 1)
                if 8 * j + dc // 2 >= 1:
                    v.wait_ge(S["sbst1" if dc % 2 else "sbst0"],
                              16 * (8 * j + dc // 2))
                v.scalar_tensor_tensor(yout[dc % 2][:], psD[dc % 2][:],
                                       ds_col[:, dc:dc + 1],
                                       y1t[:, dc * TB:(dc + 1) * TB],
                                       ALU.mult, ALU.add).then_inc(S["sbyev"], 1)

    @blk.tensor
    def _(t):
        t.wait_ge(S["barA2"], 4)
        for j in range(NTB):
            t.wait_ge(S["sbcast"], j + 1)
            t.wait_ge(S["ssqB"], j + 1)
            if j >= 1:
                t.wait_ge(S["sbvt"], j)
            for k in range(KD_):
                t.matmul(psSt[0][:], ones_bf[:], r1bf[:, k * TB:(k + 1) * TB],
                         start=(k == 0), stop=(k == KD_ - 1))
                ins = t.matmul(psSt[1][:], ones_bf[:], sqB[:, k * TB:(k + 1) * TB],
                               start=(k == 0), stop=(k == KD_ - 1))
            ins.then_inc(S["sbsm"], 1)
            t.wait_ge(S["sbxn"], j + 1)
            for ib in range(NIB):
                g = NIB * j + ib
                t.wait_ge(S["sbwl1" if ib % 2 else "sbwl0"],
                          32 * (22 * j + ib // 2 + 1))
                if g >= 2:
                    t.wait_ge(S["sbhh"], g - 1)
                for k in range(KD_):
                    t.matmul(psG[ib % 2][:],
                             gw_t[ib % 2][:, k * P:(k + 1) * P],
                             xn1[:, k * TB:(k + 1) * TB],
                             start=(k == 0), stop=(k == KD_ - 1))
                for k in range(KD_):
                    ins = t.matmul(psU[ib % 2][:],
                                   uw_t[ib % 2][:, k * P:(k + 1) * P],
                                   xn1[:, k * TB:(k + 1) * TB],
                                   start=(k == 0), stop=(k == KD_ - 1))
                ins.then_inc(S["sbgm"], 1)
            t.wait_ge(S["sbhh"], NIB * (j + 1))
            for dc in range(KD_):
                g = KD_ * j + dc
                t.wait_ge(S["sbdl1" if dc % 2 else "sbdl0"],
                          16 * (8 * j + dc // 2 + 1))
                if g >= 2:
                    t.wait_ge(S["sbyev"], g - 1)
                for ib in range(NIB):
                    ins = t.matmul(psD[dc % 2][:],
                                   dw_t[dc % 2][:, ib * P:(ib + 1) * P],
                                   hh[:, ib * TB:(ib + 1) * TB],
                                   start=(ib == 0), stop=(ib == NIB - 1))
                ins.then_inc(S["sbdm"], 1)

    @blk.scalar
    def _(sc):
        sc.wait_ge(S["barA2"], 4)
        for j in range(NTB):
            sc.wait_ge(S["sbvt"], j + 1)
            sc.activation(rtB[:], vtB[:], AF.Ln, bias=c_eps[:, 0:1])
            sc.activation(rtB[:], rtB[:], AF.Exp, scale=-0.5).then_inc(S["sbrt"], 1)
            for ib in range(NIB):
                g = NIB * j + ib
                sc.wait_ge(S["sbgm"], g + 1)
                if g >= 2:
                    sc.wait_ge(S["sbhh"], g - 1)
                sc.activation(sgbf[ib % 2][:], psG[ib % 2][:], AF.Silu,
                              scale=gs_col[:, ib:ib + 1]).then_inc(S["sbsg"], 1)
            for dc in range(KD_):
                g = KD_ * j + dc
                sc.wait_ge(S["sbyev"], g + 1)
                sc.dma_start(
                    out=out_y[dc * P:(dc + 1) * P, j * TB:(j + 1) * TB],
                    in_=yout[dc % 2][:],
                ).then_inc(S["sbst1" if dc % 2 else "sbst0"], 16)

    @blk.gpsimd
    def _(gp):
        gp.wait_ge(S["barA2"], 4)
        for j in range(NTB):
            gp.wait_ge(S["sbcast"], j + 1)
            for k in range(KD_):
                ins = gp.tensor_tensor(sqB[:, k * TB:(k + 1) * TB],
                                       r1bf[:, k * TB:(k + 1) * TB],
                                       r1bf[:, k * TB:(k + 1) * TB], ALU.mult)
            ins.then_inc(S["ssqB"], 1)
        gp.wait_ge(S["sbst0"], 16 * (NTB * KD_ // 2))
        gp.wait_ge(S["sbst1"], 16 * (NTB * KD_ // 2))

    esB.close()
    es.close()
    return nc


_NC_CACHE = {}


def kernel(L0_attn_x, L0_attn_log_decay, L0_attn_Wq, L0_attn_Wk, L0_attn_Wv,
           L0_attn_Wo, L0_attn_Wg1, L0_attn_Wg2, L0_attn_norm_w,
           L0_ffn_gate_w, L0_ffn_up_w, L0_ffn_down_w,
           L0_ffn_gate_s, L0_ffn_up_s, L0_ffn_down_s):
    f32 = np.float32
    x = np.asarray(L0_attn_x, f32)
    ldec = np.asarray(L0_attn_log_decay, f32)
    Wq = np.asarray(L0_attn_Wq, f32)
    Wk = np.asarray(L0_attn_Wk, f32)
    Wv = np.asarray(L0_attn_Wv, f32)
    Wo = np.asarray(L0_attn_Wo, f32)
    Wg1 = np.ascontiguousarray(np.asarray(L0_attn_Wg1, f32))
    Wg2 = np.asarray(L0_attn_Wg2, f32)
    nw = np.asarray(L0_attn_norm_w, f32)
    gw = np.asarray(L0_ffn_gate_w, f32)
    uw = np.asarray(L0_ffn_up_w, f32)
    dw = np.asarray(L0_ffn_down_w, f32)
    gsv = np.ascontiguousarray(np.asarray(L0_ffn_gate_s, f32))
    usv = np.ascontiguousarray(np.asarray(L0_ffn_up_s, f32))
    dsv = np.ascontiguousarray(np.asarray(L0_ffn_down_s, f32))

    xTb = [np.ascontiguousarray(x[b].T) for b in range(B)]
    xAb = [np.ascontiguousarray(
        xTb[b].reshape(4, 4, P, NTA, TA).transpose(3, 0, 2, 1, 4).reshape(NTA, 4, P, 4 * TA))
        for b in range(B)]
    gwt_t = np.ascontiguousarray(
        gw.T.reshape(KD_, P, NIB, P).transpose(2, 1, 0, 3).reshape(NIB, P, KD_ * P))
    uwt_t = np.ascontiguousarray(
        uw.T.reshape(KD_, P, NIB, P).transpose(2, 1, 0, 3).reshape(NIB, P, KD_ * P))
    dwt_t = np.ascontiguousarray(
        dw.T.reshape(NIB, P, KD_, P).transpose(2, 1, 0, 3).reshape(KD_, P, NIB * P))

    idx = np.arange(P)
    diff = idx[None, :] - idx[:, None]
    DIFF = np.where(diff >= 0, diff, 1.0e9).astype(f32)
    CP1 = np.broadcast_to((idx + 1.0)[None, :], (P, P)).astype(f32)
    KREV = np.broadcast_to((P - 1.0 - idx)[:, None], (P, P)).astype(f32)
    IDT = np.eye(P, dtype=ml_dtypes.bfloat16)

    in_maps = []
    for c in range(8):
        b, hg = c // 2, c % 2
        e0, e1 = hg * EC, (hg + 1) * EC
        def pmaj(a, nk):
            return np.ascontiguousarray(
                a.reshape(nk, P, a.shape[1]).transpose(1, 0, 2).reshape(P, -1))
        in_maps.append({
            "xA": xAb[b],
            "xTB": np.ascontiguousarray(xTb[b][:, hg * TOKB:(hg + 1) * TOKB]),
            "wqd": pmaj(np.ascontiguousarray(Wq[:, e0:e1]), KD_),
            "wkd": pmaj(np.ascontiguousarray(Wk[:, e0:e1]), KD_),
            "wvd": pmaj(np.ascontiguousarray(Wv[:, e0:e1]), KD_),
            "wg1d": pmaj(Wg1, KD_),
            "wg2d": np.ascontiguousarray(Wg2[:, e0:e1]),
            "wosd": pmaj(np.ascontiguousarray(Wo[e0:e1, :]), HPC),
            "gwTd": gwt_t,
            "uwTd": uwt_t,
            "dwTd": dwt_t,
            "ld8": np.ascontiguousarray(ldec[hg * HPC:(hg + 1) * HPC]),
            "gsd": gsv,
            "usd": usv,
            "dsd": dsv,
            "nwsd": np.ascontiguousarray(nw[e0:e1]),
            "cDIFF": DIFF,
            "cCP1": CP1,
            "cKREV": KREV,
            "cIDT": IDT,
        })

    nc = build_program()
    res = run_bass_kernel_spmd(nc, in_maps, core_ids=list(range(8)))

    out = np.empty((B, N, D), f32)
    for b in range(B):
        left = np.asarray(res.results[2 * b]["out_y"])
        right = np.asarray(res.results[2 * b + 1]["out_y"])
        out[b, 0:TOKB, :] = left.T
        out[b, TOKB:N, :] = right.T
    return out
